# version 19
# speedup vs baseline: 1.2608x; 1.0004x over previous
import numpy as np
from contextlib import ExitStack

import concourse.bass as bass
import concourse.mybir as mybir
from concourse.bass_utils import run_bass_kernel_spmd

F32 = mybir.dt.float32
BF16 = mybir.dt.bfloat16
AF = mybir.ActivationFunctionType
ALU = mybir.AluOpType
AXX = mybir.AxisListType.X

NCORE = 8
D = 1024
NCH = 8
NT = 4096
TW = 512
NTILE = 8
NB = 8
PL = 512
NEXP = 64
FEXP = 512
TOPK = 8
NEG_BIG = -1e30


class Buf:
    __slots__ = ("name", "writer", "readers", "pending")
    def __init__(self, name=""):
        self.name = name
        self.writer = None
        self.readers = {}
        self.pending = False

ENGS = ("sync", "tensor", "vector", "scalar", "gpsimd")
ENG_SEM = {"tensor": "sP", "vector": "sV", "scalar": "sA", "gpsimd": "sG",
           "sync": "sS"}
QUEUES = {"q": ("sync", ["q0", "q1", "q2", "q3"]),
          "a": ("scalar", ["a0", "a1"]),
          "g": ("gpsimd", ["g0", "g1", "g2", "g3"])}


class Prog:
    def __init__(self, nc, ctx):
        self.nc = nc
        self.ctx = ctx
        self.streams = {e: [] for e in ENGS}
        self.counts = {}
        self.waited = {e: {} for e in ENGS}
        self.sems = {}
        self.qidx = {q: 0 for q in QUEUES}
        self.pending_bufs = {e: [] for e in ENGS}
        self.pending_reads = {e: [] for e in ENGS}
        names = list(ENG_SEM.values()) + sum((p for _, p in QUEUES.values()), []) + ["cc"]
        for n in names:
            self.sems[n] = ctx.enter_context(nc.semaphore(n))
            self.counts[n] = 0

    def _emit(self, eng, fn):
        self.streams[eng].append(fn)

    def _wait(self, eng, sname, val):
        if val <= 0:
            return
        if self.waited[eng].get(sname, 0) >= val:
            return
        self.waited[eng][sname] = val
        h = self.sems[sname]
        self._emit(eng, lambda E, h=h, v=val: E.wait_ge(h, v))

    def _dep_waits(self, eng, reads, writes):
        for b in reads:
            if b.pending:
                assert b in self.pending_bufs[eng], \
                    f"cross-engine read of pending buf {b.name}"
                continue
            if b.writer:
                self._wait(eng, *b.writer)
        for b in writes:
            if b.pending:
                assert b in self.pending_bufs[eng], \
                    f"cross-engine write of pending buf {b.name}"
                continue
            if b.writer:
                self._wait(eng, *b.writer)
            for s, v in b.readers.items():
                self._wait(eng, s, v)

    def _finish(self, token, reads, writes):
        s, v = token
        for b in reads:
            if b.readers.get(s, 0) < v:
                b.readers[s] = v
        for b in writes:
            b.writer = token
            b.readers = {}
            b.pending = False

    def op(self, eng, fn, reads=(), writes=(), rw=()):
        reads = list(reads) + list(rw)
        writes = list(writes) + list(rw)
        self._dep_waits(eng, reads, writes)
        s = ENG_SEM[eng]
        self._wait(eng, s, self.counts[s])
        self.counts[s] += 1
        tgt = self.counts[s]
        h = self.sems[s]
        self._emit(eng, lambda E, fn=fn, h=h: fn(E).then_inc(h, 1))
        token = (s, tgt)
        for b in self.pending_bufs[eng]:
            b.writer = token
            b.readers = {}
            b.pending = False
        self.pending_bufs[eng] = []
        for b in self.pending_reads[eng]:
            if b.readers.get(s, 0) < tgt:
                b.readers[s] = tgt
        self.pending_reads[eng] = []
        self._finish(token, reads, writes)
        return token

    def gop(self, eng, fn, reads=(), writes=(), rw=()):
        reads = list(reads) + list(rw)
        writes = list(writes) + list(rw)
        self._dep_waits(eng, reads, writes)
        self._emit(eng, fn)
        for b in writes:
            b.pending = True
            if b not in self.pending_bufs[eng]:
                self.pending_bufs[eng].append(b)
        for b in reads:
            if b not in self.pending_reads[eng]:
                self.pending_reads[eng].append(b)

    def dma(self, q, fn, reads=(), writes=(), rw=()):
        eng, pool = QUEUES[q]
        reads = list(reads) + list(rw)
        writes = list(writes) + list(rw)
        self._dep_waits(eng, reads, writes)
        s = pool[self.qidx[q] % len(pool)]
        self.qidx[q] += 1
        self._wait(eng, s, self.counts[s])
        self.counts[s] += 16
        tgt = self.counts[s]
        h = self.sems[s]
        self._emit(eng, lambda E, fn=fn, h=h: fn(E).then_inc(h, 16))
        token = (s, tgt)
        self._finish(token, reads, writes)
        return token

    def barrier_token(self, q):
        return [(s, self.counts[s]) for s in QUEUES[q][1] if self.counts[s]]

    def barrier(self):
        for e in ENGS:
            assert not self.pending_bufs[e] and not self.pending_reads[e]
        for e in ENGS:
            for s, v in sorted(self.counts.items()):
                if v:
                    self._wait(e, s, v)

    def cst(self, val):
        if not hasattr(self, "_consts"):
            self._consts = {}
        if val not in self._consts:
            i = len(self._consts)
            t = self.ctx.enter_context(
                self.nc.sbuf_tensor(f"cst{i}", [1, 1], F32))
            b = Buf(f"cst{val}")
            self.op("vector", lambda E, t=t, val=val: E.memset(t[0:1, 0:1], val),
                    writes=[b])
            self.nc.const_aps.register(F32, float(val), t[0:1, 0:1])
            self._consts[val] = b
        return self._consts[val]

    def finalize(self, out_tokens):
        nc = self.nc
        with nc.Block() as block:
            for eng in ENGS:
                stream = self.streams[eng]
                if eng == "sync":
                    extra = []
                    for s, v in out_tokens:
                        h = self.sems[s]
                        extra.append(lambda E, h=h, v=v: E.wait_ge(h, v))
                    stream = stream + extra
                if not stream:
                    continue
                sec = getattr(block, eng)
                def run(E, stream=stream):
                    for f in stream:
                        f(E)
                sec(run)


def ap(t, offset, dims):
    return bass.AP(t, offset, [list(d) for d in dims])



L0_KEYS = ["x", "halo", "bwT", "wrT", "wkT", "wvT", "cwT", "logA", "dpar",
           "mur", "muk", "onec", "hpick", "x1", "z0", "cardr", "carga"]

def _decl_l0_drams(nc, fused=False):
    T = {}
    T["x"] = nc.dram_tensor("x", [128, NTILE * NCH * TW], F32, kind="ExternalInput")
    T["halo"] = nc.dram_tensor("halo", [128, NCH * NB], F32, kind="ExternalInput")
    T["bwT"] = nc.dram_tensor("bwT", [D, 128], F32, kind="ExternalInput")
    T["wrT"] = nc.dram_tensor("wrT", [D, D], F32, kind="ExternalInput")
    T["wkT"] = nc.dram_tensor("wkT", [D, D], F32, kind="ExternalInput")
    T["wvT"] = nc.dram_tensor("wvT", [D, D], F32, kind="ExternalInput")
    T["cwT"] = nc.dram_tensor("cwT", [128, D], F32, kind="ExternalInput")
    T["logA"] = nc.dram_tensor("logA", [128, 1], F32, kind="ExternalInput")
    T["dpar"] = nc.dram_tensor("dpar", [128, NCH], F32, kind="ExternalInput")
    T["mur"] = nc.dram_tensor("mur", [128, NCH], F32, kind="ExternalInput")
    T["muk"] = nc.dram_tensor("muk", [128, NCH], F32, kind="ExternalInput")
    T["onec"] = nc.dram_tensor("onec", [128, 1], F32, kind="ExternalInput")
    T["hpick"] = nc.dram_tensor("hpick", [128, NCORE * NB], F32, kind="ExternalInput")
    T["x1"] = nc.dram_tensor("x1", [128, NTILE * NCH * TW], F32,
                             kind="Internal" if fused else "ExternalOutput")
    T["z0"] = nc.dram_tensor("z0", [128, NTILE * NCH * TW], F32, kind="Internal")
    T["cardr"] = nc.dram_tensor("cardr", [128, NB], F32, kind="Internal")
    T["carga"] = nc.dram_tensor("carga", [NCORE * 128, NB], F32, kind="Internal")
    return T


def emit_l0(nc, p, T, ctx, pre="l0_"):
    x_h = T["x"]; halo_h = T["halo"]; bwT_h = T["bwT"]; wrT_h = T["wrT"]
    wkT_h = T["wkT"]; wvT_h = T["wvT"]; cwT_h = T["cwT"]; logA_h = T["logA"]
    dpar_h = T["dpar"]; mur_h = T["mur"]; muk_h = T["muk"]
    onec_h = T["onec"]; hpick_h = T["hpick"]; x1_h = T["x1"]; z0_h = T["z0"]
    cardr_h = T["cardr"]; carga_h = T["carga"]

    sb = lambda name, shape, dt: ctx.enter_context(
        nc.sbuf_tensor(pre + name, shape, dt))
    psb = lambda name, shape, dt=F32: ctx.enter_context(
        nc.psum_tensor(pre + name, shape, dt))

    bwT = sb("bwT_b", [128, NCH * 128], BF16)
    wrT = sb("wrT_b", [128, NCH * D], BF16)
    wkT = sb("wkT_b", [128, NCH * D], BF16)
    wvT = sb("wvT_b", [128, NCH * D], BF16)
    cwT = sb("cwT_b", [128, D], F32)
    logA = sb("logA_s", [128, 1], F32)
    dpar = sb("dpar_s", [128, NCH], F32)
    mur = sb("mur_s", [128, NCH], F32)
    muk = sb("muk_s", [128, NCH], F32)
    onec = sb("onec_s", [128, 1], F32)
    hpick = sb("hpick_s", [128, NCORE * NB], F32)
    halo = sb("halo_s", [128, NCH * NB], F32)
    haloxn = sb("haloxn_s", [128, NCH * NB], F32)

    xsl = [sb(f"xsl{i}", [128, NCH * TW], F32) for i in range(2)]
    sq = sb("sq", [128, 2 * TW], F32)
    xn5 = [sb(f"xn5_{i}", [128, NCH * (TW + 1)], BF16) for i in range(2)]
    xrb = [sb(f"xrb{i}", [128, NCH * TW], BF16) for i in range(2)]
    xkb = [sb(f"xkb{i}", [128, NCH * TW], BF16) for i in range(2)]
    kab = sb("kab", [128, NCH * TW], BF16)
    rsl = sb("rsl", [128, 2 * TW], F32)
    dif = sb("dif", [128, TW], BF16)
    tmpv = sb("tmpv", [128, TW], F32)
    mean = sb("mean", [1, TW], F32)
    msq = sb("msq", [1, TW], F32)
    var = sb("var", [1, TW], F32)
    varl = sb("varl", [1, TW], F32)
    rstd = sb("rstd", [1, TW], F32)
    mrs = sb("mrs", [1, TW], F32)
    rstd_bc = sb("rstd_bc", [128, TW], F32)
    mrs_bc = sb("mrs_bc", [128, TW], F32)
    hrow = sb("hrow", [1, NB], F32)
    hrow2 = sb("hrow2", [1, NB], F32)
    hvar = sb("hvar", [1, NB], F32)
    hrstd = sb("hrstd", [1, NB], F32)
    hmrs = sb("hmrs", [1, NB], F32)
    hrstd_bc = sb("hrstd_bc", [128, NB], F32)
    hmrs_bc = sb("hmrs_bc", [128, NB], F32)
    xp = sb("xp", [128, NT], F32)
    acol = sb("acol", [128, 1], F32)
    a512 = sb("a512", [128, 1], F32)
    apow = sb("apow", [128, TW], F32)
    carry = sb("carry", [128, NB], F32)
    carga = sb("carga_s", [128, NCORE * NB], F32)
    hscan = sb("hscan", [128, NCORE * NB], F32)
    hin = sb("hin", [128, NB], F32)

    ps_su = psb("ps_su", [1, TW])
    ps_sq = psb("ps_sq", [1, TW])
    ps_hsu = psb("ps_hsu", [1, NCH * NB])
    ps_hsq = psb("ps_hsq", [1, NCH * NB])
    ps_xp = psb("ps_xp", [128, TW])
    ps_m0 = psb("ps_m0", [128, TW])
    ps_m1 = psb("ps_m1", [128, TW])


    B = lambda n: Buf(n)
    b_w = {n: B(n) for n in ["bwT", "wrT", "wkT", "wvT", "cwT", "logA", "dpar",
                             "mur", "muk", "onec", "hpick", "halo"]}
    b_haloxn = B("haloxn")
    b_hrows = B("hrows")
    b_xsl = [[B(f"xsl{i}s{k}") for k in range(4)] for i in range(2)]
    b_sq = [B(f"sq{i}") for i in range(2)]
    b_xn5 = [[B(f"xn5_{i}h{h}") for h in range(2)] for i in range(2)]
    b_xrb = [B(f"xrb{i}") for i in range(2)]
    b_xkb = [B(f"xkb{i}") for i in range(2)]
    b_kab = [B(f"kab{i}") for i in range(NCH)]
    b_rsl = [B(f"rsl{i}") for i in range(2)]
    b_dif = B("dif")
    b_tmpv = B("tmpv")
    b_stat = B("stat")
    b_bc = B("bc")
    b_xp = B("xp")
    b_acol = B("acol")
    b_apow = B("apow")
    b_carry = B("carry")
    b_carga = B("carga")
    b_hscan = B("hscan")
    b_hin = B("hin")
    b_ps_su = B("ps_su")
    b_ps_sq = B("ps_sq")
    b_ps_hsu = B("ps_hsu")
    b_ps_hsq = B("ps_hsq")
    b_ps_xp = B("ps_xp")
    b_ps_m = [B("ps_m0"), B("ps_m1")]
    b_z0 = [[B(f"z0_{j}s{k}") for k in range(4)] for j in range(NTILE)]
    b_x1 = [[B(f"x1_{j}s{k}") for k in range(4)] for j in range(NTILE)]
    b_cardr = B("cardr")
    b_carga_d = B("carga_d")

    p.dma("g", lambda E: E.dma_start(
        out=ap(bwT, 0, [[NCH * 128, 128], [128, NCH], [1, 128]]),
        in_=ap(bwT_h, 0, [[128, 128], [128 * 128, NCH], [1, 128]])),
        writes=[b_w["bwT"]])
    for name, hb, sbuf, key in [("wrT", wrT_h, wrT, "wrT"),
                                ("wkT", wkT_h, wkT, "wkT"),
                                ("wvT", wvT_h, wvT, "wvT")]:
        p.dma("g", lambda E, hb=hb, sbuf=sbuf: E.dma_start(
            out=ap(sbuf, 0, [[NCH * D, 128], [D, NCH], [1, D]]),
            in_=ap(hb, 0, [[D, 128], [128 * D, NCH], [1, D]])),
            writes=[b_w[key]])
    p.dma("q", lambda E: E.dma_start(out=cwT[:, :], in_=cwT_h[:, :]),
          writes=[b_w["cwT"]])
    for hb, sbuf, key in [(logA_h, logA, "logA"), (dpar_h, dpar, "dpar"),
                          (mur_h, mur, "mur"), (muk_h, muk, "muk"),
                          (onec_h, onec, "onec"),
                          (hpick_h, hpick, "hpick"), (halo_h, halo, "halo")]:
        p.dma("q", lambda E, hb=hb, sbuf=sbuf: E.dma_start(
            out=sbuf[:, :], in_=hb[:, :]), writes=[b_w[key]])

    p.op("tensor", lambda E: E.matmul(ps_hsu[0:1, :], onec[:, 0:1],
                                      halo[:, :], start=True, stop=True),
         reads=[b_w["onec"], b_w["halo"]], writes=[b_ps_hsu])
    p.op("scalar", lambda E: E.activation(sq[:, 0:NCH * NB], halo[:, :],
                                          AF.Square),
         reads=[b_w["halo"]], writes=[b_sq[0]])
    p.op("tensor", lambda E: E.matmul(ps_hsq[0:1, :], onec[:, 0:1],
                                      sq[:, 0:NCH * NB], start=True, stop=True),
         reads=[b_w["onec"], b_sq[0]], writes=[b_ps_hsq])
    p.op("vector", lambda E: E.reduce_sum(
        hrow[0:1, :], ap(ps_hsu, 0, [[NB * NCH, 1], [1, NB], [NB, NCH]]),
        axis=AXX), reads=[b_ps_hsu], writes=[b_hrows])
    p.op("vector", lambda E: E.reduce_sum(
        hrow2[0:1, :], ap(ps_hsq, 0, [[NB * NCH, 1], [1, NB], [NB, NCH]]),
        axis=AXX), reads=[b_ps_hsq], writes=[b_hrows])
    p.op("scalar", lambda E: E.activation(hrow[0:1, :], hrow[0:1, :], AF.Copy,
                                          scale=1.0 / D),
         reads=[p.cst(1.0 / D)], rw=[b_hrows])
    p.op("scalar", lambda E: E.activation(hrow2[0:1, :], hrow2[0:1, :], AF.Copy,
                                          scale=1.0 / D),
         reads=[p.cst(1.0 / D)], rw=[b_hrows])
    p.op("vector", lambda E: E.tensor_mul(hvar[0:1, :], hrow[0:1, :],
                                          hrow[0:1, :]),
         reads=[b_hrows], writes=[b_hrows])
    p.op("vector", lambda E: E.tensor_sub(hvar[0:1, :], hrow2[0:1, :],
                                          hvar[0:1, :]), rw=[b_hrows])
    p.op("scalar", lambda E: E.activation(hvar[0:1, :], hvar[0:1, :], AF.Ln,
                                          bias=1e-5),
         reads=[p.cst(1e-5)], rw=[b_hrows])
    p.op("scalar", lambda E: E.activation(hrstd[0:1, :], hvar[0:1, :], AF.Exp,
                                          scale=-0.5),
         reads=[p.cst(-0.5)], rw=[b_hrows])
    p.op("vector", lambda E: E.tensor_mul(hmrs[0:1, :], hrow[0:1, :],
                                          hrstd[0:1, :]), rw=[b_hrows])
    p.dma("q", lambda E: E.dma_start(out=hrstd_bc[:, :],
                                     in_=ap(hrstd, 0, [[NB, 1], [0, 128], [1, NB]])),
          reads=[b_hrows], writes=[b_hrows])
    p.dma("q", lambda E: E.dma_start(out=hmrs_bc[:, :],
                                     in_=ap(hmrs, 0, [[NB, 1], [0, 128], [1, NB]])),
          reads=[b_hrows], writes=[b_hrows])
    p.op("vector", lambda E: E.tensor_mul(
        haloxn[:, :], halo[:, :],
        ap(hrstd_bc, 0, [[NB, 128], [0, NCH], [1, NB]])),
        reads=[b_w["halo"], b_hrows], writes=[b_haloxn])
    p.op("vector", lambda E: E.tensor_sub(
        haloxn[:, :], haloxn[:, :],
        ap(hmrs_bc, 0, [[NB, 128], [0, NCH], [1, NB]])),
        reads=[b_hrows], rw=[b_haloxn])

    def stage_ln_stats(j):
        par = j % 2
        xs, bxs = xsl[par], b_xsl[par]
        js = j * TW
        for k in range(4):
            p.dma("q", lambda E, js=js, xs=xs, k=k: E.dma_start(
                out=ap(xs, 2 * k * TW, [[NCH * TW, 128], [1, 2 * TW]]),
                in_=ap(x_h, js * NCH + 2 * k * TW,
                       [[NTILE * NCH * TW, 128], [1, 2 * TW]])),
                writes=[bxs[k]])
        for ci in range(NCH):
            fn = lambda E, ci=ci, xs=xs: E.matmul(
                ps_su[0:1, :], onec[:, 0:1], xs[:, ci * TW:(ci + 1) * TW],
                start=(ci == 0), stop=(ci == NCH - 1))
            if ci < NCH - 1:
                p.gop("tensor", fn, reads=[b_w["onec"], bxs[ci // 2]], writes=[b_ps_su])
            else:
                p.op("tensor", fn, reads=[b_w["onec"], bxs[ci // 2]], writes=[b_ps_su])
        for ci in range(NCH):
            sl = ci % 2
            p.op("scalar", lambda E, ci=ci, sl=sl, xs=xs: E.activation(
                sq[:, sl * TW:(sl + 1) * TW], xs[:, ci * TW:(ci + 1) * TW],
                AF.Square), reads=[bxs[ci // 2]], writes=[b_sq[sl]])
            p.op("tensor", lambda E, ci=ci, sl=sl: E.matmul(
                ps_sq[0:1, :], onec[:, 0:1], sq[:, sl * TW:(sl + 1) * TW],
                start=(ci == 0), stop=(ci == NCH - 1)),
                reads=[b_w["onec"], b_sq[sl]], writes=[b_ps_sq])
        p.op("scalar", lambda E: E.activation(mean[0:1, :], ps_su[0:1, :],
                                              AF.Copy, scale=1.0 / D),
             reads=[b_ps_su, p.cst(1.0 / D)], writes=[b_stat])
        p.op("scalar", lambda E: E.activation(msq[0:1, :], ps_sq[0:1, :],
                                              AF.Copy, scale=1.0 / D),
             reads=[b_ps_sq, p.cst(1.0 / D)], writes=[b_stat])
        p.op("vector", lambda E: E.tensor_mul(var[0:1, :], mean[0:1, :],
                                              mean[0:1, :]), rw=[b_stat])
        p.op("vector", lambda E: E.tensor_sub(var[0:1, :], msq[0:1, :],
                                              var[0:1, :]), rw=[b_stat])
        p.op("scalar", lambda E: E.activation(varl[0:1, :], var[0:1, :],
                                              AF.Ln, bias=1e-5),
             reads=[p.cst(1e-5)], rw=[b_stat])
        p.op("scalar", lambda E: E.activation(rstd[0:1, :], varl[0:1, :],
                                              AF.Exp, scale=-0.5),
             reads=[p.cst(-0.5)], rw=[b_stat])
        p.op("vector", lambda E: E.tensor_mul(mrs[0:1, :], mean[0:1, :],
                                              rstd[0:1, :]), rw=[b_stat])
        p.dma("q", lambda E: E.dma_start(
            out=rstd_bc[:, :], in_=ap(rstd, 0, [[TW, 1], [0, 128], [1, TW]])),
            reads=[b_stat], writes=[b_bc])
        p.dma("q", lambda E: E.dma_start(
            out=mrs_bc[:, :], in_=ap(mrs, 0, [[TW, 1], [0, 128], [1, TW]])),
            reads=[b_stat], writes=[b_bc])
    def stage_ln_apply(j):
        par = j % 2
        xs, bxs = xsl[par], b_xsl[par]
        x5 = xn5[par]
        for ci in range(NCH):
            c5 = ci * (TW + 1)
            half = ci // 4
            eng = "gpsimd" if half == 0 else "vector"
            bx5 = b_xn5[par][half]
            p.op(eng, lambda E, ci=ci, c5=c5, xs=xs, x5=x5: E.tensor_mul(
                x5[:, c5 + 1:c5 + 1 + TW], xs[:, ci * TW:(ci + 1) * TW],
                rstd_bc[:, :]), reads=[bxs[ci // 2], b_bc], writes=[bx5])
            p.op(eng, lambda E, ci=ci, c5=c5, x5=x5: E.tensor_sub(
                x5[:, c5 + 1:c5 + 1 + TW], x5[:, c5 + 1:c5 + 1 + TW],
                mrs_bc[:, :]), reads=[b_bc], rw=[bx5])
            p.op(eng, lambda E, ci=ci, c5=c5, j=j, x5=x5: E.tensor_copy(
                x5[:, c5:c5 + 1], haloxn[:, ci * NB + j:ci * NB + j + 1]),
                reads=[b_haloxn], writes=[bx5])
        for ci in range(NCH):
            c5 = ci * (TW + 1)
            bx5 = b_xn5[par][ci // 4]
            p.op("vector", lambda E, ci=ci, c5=c5, xs=xs, x5=x5: E.scalar_tensor_tensor(
                xs[:, ci * TW:(ci + 1) * TW], x5[:, c5 + 1:c5 + 1 + TW],
                dpar[:, ci:ci + 1], xs[:, ci * TW:(ci + 1) * TW],
                ALU.mult, ALU.add),
                reads=[bx5, b_w["dpar"]], rw=[bxs[ci // 2]])
        for ci in range(NCH):
            c5 = ci * (TW + 1)
            bx5 = b_xn5[par][ci // 4]
            p.op("vector", lambda E, ci=ci, c5=c5, x5=x5: E.tensor_sub(
                dif[:, :], x5[:, c5 + 1:c5 + 1 + TW], x5[:, c5:c5 + TW]),
                reads=[bx5], writes=[b_dif])
            p.op("vector", lambda E, ci=ci, c5=c5, par=par, x5=x5: E.scalar_tensor_tensor(
                xrb[par][:, ci * TW:(ci + 1) * TW], dif[:, :], mur[:, ci:ci + 1],
                x5[:, c5:c5 + TW], ALU.mult, ALU.add),
                reads=[b_dif, b_w["mur"], bx5], writes=[b_xrb[par]])
            p.op("vector", lambda E, ci=ci, c5=c5, par=par, x5=x5: E.scalar_tensor_tensor(
                xkb[par][:, ci * TW:(ci + 1) * TW], dif[:, :], muk[:, ci:ci + 1],
                x5[:, c5:c5 + TW], ALU.mult, ALU.add),
                reads=[b_dif, b_w["muk"], bx5], writes=[b_xkb[par]])

    def stage_mm(j):
        par = j % 2
        xs, bxs = xsl[par], b_xsl[par]
        js = j * TW
        x5 = xn5[par]
        for ci in range(NCH):
            c5 = ci * (TW + 1)
            bx5 = b_xn5[par][ci // 4]
            fn = lambda E, ci=ci, c5=c5, x5=x5: E.matmul(
                ps_xp[:, :], bwT[:, ci * 128:(ci + 1) * 128],
                x5[:, c5 + 1:c5 + 1 + TW],
                start=(ci == 0), stop=(ci == NCH - 1))
            if ci < NCH - 1:
                p.gop("tensor", fn, reads=[b_w["bwT"], bx5], writes=[b_ps_xp])
            else:
                p.op("tensor", fn, reads=[b_w["bwT"], bx5], writes=[b_ps_xp])
        p.op("vector", lambda E, js=js: E.tensor_copy(
            xp[:, js:js + TW], ps_xp[:, :]), reads=[b_ps_xp], writes=[b_xp])
        for dj in range(NCH):
            pm, bpm = (ps_m0, b_ps_m[0]) if dj % 2 == 0 else (ps_m1, b_ps_m[1])
            for ci in range(NCH):
                fn = lambda E, ci=ci, dj=dj, pm=pm, par=par: E.matmul(
                    pm[:, :], wkT[:, ci * D + dj * 128:ci * D + (dj + 1) * 128],
                    xkb[par][:, ci * TW:(ci + 1) * TW],
                    start=(ci == 0), stop=(ci == NCH - 1))
                if ci < NCH - 1:
                    p.gop("tensor", fn, reads=[b_w["wkT"], b_xkb[par]], writes=[bpm])
                else:
                    p.op("tensor", fn, reads=[b_w["wkT"], b_xkb[par]], writes=[bpm])
            p.op("scalar", lambda E, pm=pm: E.activation(
                tmpv[:, :], pm[:, :], AF.Relu), reads=[bpm], writes=[b_tmpv])
            p.op("scalar", lambda E, dj=dj: E.activation(
                kab[:, dj * TW:(dj + 1) * TW], tmpv[:, :], AF.Square),
                reads=[b_tmpv], writes=[b_kab[dj]])
        for dj in range(NCH):
            sl = dj % 2
            pmr, bpmr = (ps_m0, b_ps_m[0]) if sl == 0 else (ps_m1, b_ps_m[1])
            pmv, bpmv = (ps_m1, b_ps_m[1]) if sl == 0 else (ps_m0, b_ps_m[0])
            for ci in range(NCH):
                fn = lambda E, ci=ci, dj=dj, pm=pmr, par=par: E.matmul(
                    pm[:, :], wrT[:, ci * D + dj * 128:ci * D + (dj + 1) * 128],
                    xrb[par][:, ci * TW:(ci + 1) * TW],
                    start=(ci == 0), stop=(ci == NCH - 1))
                if ci < NCH - 1:
                    p.gop("tensor", fn, reads=[b_w["wrT"], b_xrb[par]], writes=[bpmr])
                else:
                    p.op("tensor", fn, reads=[b_w["wrT"], b_xrb[par]], writes=[bpmr])
            p.op("scalar", lambda E, sl=sl, pm=pmr: E.activation(
                rsl[:, sl * TW:(sl + 1) * TW], pm[:, :], AF.Sigmoid),
                reads=[bpmr], writes=[b_rsl[sl]])
            for ci in range(NCH):
                fn = lambda E, ci=ci, dj=dj, pm=pmv: E.matmul(
                    pm[:, :], wvT[:, ci * D + dj * 128:ci * D + (dj + 1) * 128],
                    kab[:, ci * TW:(ci + 1) * TW],
                    start=(ci == 0), stop=(ci == NCH - 1))
                if ci < NCH - 1:
                    p.gop("tensor", fn, reads=[b_w["wvT"], b_kab[ci]], writes=[bpmv])
                else:
                    p.op("tensor", fn, reads=[b_w["wvT"], b_kab[ci]], writes=[bpmv])
            p.op("vector", lambda E, pm=pmv, sl=sl: E.tensor_mul(
                tmpv[:, :], pm[:, :], rsl[:, sl * TW:(sl + 1) * TW]),
                reads=[bpmv, b_rsl[sl]], writes=[b_tmpv])
            p.op("vector", lambda E, dj=dj, xs=xs: E.tensor_add(
                xs[:, dj * TW:(dj + 1) * TW], xs[:, dj * TW:(dj + 1) * TW],
                tmpv[:, :]), reads=[b_tmpv], rw=[bxs[dj // 2]])
        for k, qq in enumerate(("a", "a", "g", "g")):
            p.dma(qq, lambda E, js=js, xs=xs, j=j, k=k: E.dma_start(
                out=ap(z0_h, js * NCH + 2 * k * TW,
                       [[NTILE * NCH * TW, 128], [1, 2 * TW]]),
                in_=ap(xs, 2 * k * TW, [[NCH * TW, 128], [1, 2 * TW]])),
                reads=[bxs[k]], writes=[b_z0[j][k]])

    stage_ln_stats(0)
    stage_ln_apply(0)
    for j in range(NTILE):
        if j + 1 < NTILE:
            stage_ln_stats(j + 1)
            stage_ln_apply(j + 1)
        stage_mm(j)

    p.op("scalar", lambda E: E.activation(acol[:, :], logA[:, :], AF.Exp),
         reads=[b_w["logA"]], writes=[b_acol])
    p.op("scalar", lambda E: E.activation(a512[:, :], logA[:, :], AF.Exp,
                                          scale=512.0),
         reads=[b_w["logA"], p.cst(512.0)], writes=[b_acol])
    p.op("vector", lambda E: E.tensor_tensor_scan(
        apow[:, :], ap(acol, 0, [[1, 128], [0, TW]]),
        ap(acol, 0, [[1, 128], [0, TW]]), 1.0, ALU.mult, ALU.bypass),
        reads=[b_acol], writes=[b_apow])
    for b in range(NB):
        p.op("vector", lambda E, b=b: E.tensor_tensor_scan(
            xp[:, b * PL:(b + 1) * PL],
            ap(acol, 0, [[1, 128], [0, PL]]),
            xp[:, b * PL:(b + 1) * PL], 0.0, ALU.mult, ALU.add),
            reads=[b_acol], rw=[b_xp])
    for b in range(NB):
        p.op("vector", lambda E, b=b: E.tensor_copy(
            carry[:, b:b + 1], xp[:, b * PL + PL - 1:b * PL + PL]),
            reads=[b_xp], writes=[b_carry])
    p.dma("g", lambda E: E.dma_start(out=cardr_h[:, :], in_=carry[:, :]),
          reads=[b_carry], writes=[b_cardr])
    p.op("gpsimd", lambda E: E.collective_compute(
        "AllGather", ALU.bypass, replica_groups=[list(range(NCORE))],
        ins=[cardr_h.ap()], outs=[carga_h.ap()]),
        reads=[b_cardr], writes=[b_carga_d])
    p.dma("g", lambda E: E.dma_start(
        out=carga[:, :],
        in_=ap(carga_h, 0, [[NB, 128], [128 * NB, NCORE], [1, NB]])),
        reads=[b_carga_d], writes=[b_carga])
    for b in range(NB):
        p.op("vector", lambda E, b=b: E.tensor_tensor_scan(
            ap(hscan, b, [[NCORE * NB, 128], [NB, NCORE]]),
            ap(a512, 0, [[1, 128], [0, NCORE]]),
            ap(carga, b, [[NCORE * NB, 128], [NB, NCORE]]),
            0.0, ALU.mult, ALU.add),
            reads=[b_acol, b_carga], writes=[b_hscan])
    p.op("vector", lambda E: E.tensor_mul(hscan[:, :], hscan[:, :],
                                          hpick[:, :]),
         reads=[b_w["hpick"]], rw=[b_hscan])
    for b in range(NB):
        p.op("vector", lambda E, b=b: E.reduce_sum(
            hin[:, b:b + 1], ap(hscan, b, [[NCORE * NB, 128], [NB, NCORE]]),
            axis=AXX), reads=[b_hscan], writes=[b_hin])
    for b in range(NB):
        p.op("vector", lambda E, b=b: E.scalar_tensor_tensor(
            xp[:, b * PL:(b + 1) * PL], apow[:, :], hin[:, b:b + 1],
            xp[:, b * PL:(b + 1) * PL], ALU.mult, ALU.add),
            reads=[b_apow, b_hin], rw=[b_xp])


    for j in range(NTILE):
        js = j * TW
        par = j % 2
        xs, bxs = xsl[par], b_xsl[par]
        for k in range(4):
            p.dma("q", lambda E, js=js, xs=xs, k=k: E.dma_start(
                out=ap(xs, 2 * k * TW, [[NCH * TW, 128], [1, 2 * TW]]),
                in_=ap(z0_h, js * NCH + 2 * k * TW,
                       [[NTILE * NCH * TW, 128], [1, 2 * TW]])),
                reads=[b_z0[j][k]], writes=[bxs[k]])
        for dj in range(NCH):
            pm, bpm = (ps_m0, b_ps_m[0]) if dj % 2 == 0 else (ps_m1, b_ps_m[1])
            p.op("tensor", lambda E, dj=dj, js=js, pm=pm: E.matmul(
                pm[:, :], cwT[:, dj * 128:(dj + 1) * 128],
                xp[:, js:js + TW], start=True, stop=True),
                reads=[b_w["cwT"], b_xp], writes=[bpm])
            p.op("vector", lambda E, dj=dj, pm=pm, xs=xs: E.scalar_tensor_tensor(
                xs[:, dj * TW:(dj + 1) * TW], pm[:, :], 1.0,
                xs[:, dj * TW:(dj + 1) * TW], ALU.mult, ALU.add),
                reads=[bpm], rw=[bxs[dj // 2]])
        for k, qq in enumerate(("a", "a", "g", "g")):
            p.dma(qq, lambda E, js=js, xs=xs, k=k: E.dma_start(
                out=ap(x1_h, js * NCH + 2 * k * TW,
                       [[NTILE * NCH * TW, 128], [1, 2 * TW]]),
                in_=ap(xs, 2 * k * TW, [[NCH * TW, 128], [1, 2 * TW]])),
                reads=[bxs[k]], writes=[b_x1[j][k]])

    return [b_x1[j][k].writer for j in range(NTILE) for k in range(4)]


def build_l0():
    nc = bass.Bass("TRN2", target_bir_lowering=False, debug=False)
    ctx = ExitStack()
    T = _decl_l0_drams(nc, fused=False)
    p = Prog(nc, ctx)
    toks = emit_l0(nc, p, T, ctx)
    p.finalize(toks)
    ctx.close()
    return nc



L1A_KEYS = ["x1", "W", "proj", "ccT", "cb", "wsup", "wsdn", "cent",
            "bias_bc", "onec", "ident", "z", "xn1b", "cwm"]

def _decl_l1a_drams(nc, fused=False, T=None):
    if T is None:
        T = {}
    T["W"] = nc.dram_tensor("W", [D, D], F32, kind="ExternalInput")
    T["proj"] = nc.dram_tensor("proj", [128, NCH], F32, kind="ExternalInput")
    T["ccT"] = nc.dram_tensor("ccT", [2 * D, D], F32, kind="ExternalInput")
    T["cb"] = nc.dram_tensor("cb", [128, NCH], F32, kind="ExternalInput")
    T["wsup"] = nc.dram_tensor("wsup", [D, FEXP], F32, kind="ExternalInput")
    T["wsdn"] = nc.dram_tensor("wsdn", [FEXP, D], F32, kind="ExternalInput")
    T["cent"] = nc.dram_tensor("cent", [D, NEXP], F32, kind="ExternalInput")
    T["bias_bc"] = nc.dram_tensor("bias_bc", [128, NEXP], F32, kind="ExternalInput")
    T["ident"] = nc.dram_tensor("ident", [128, 128], F32, kind="ExternalInput")
    ko = "Internal" if fused else "ExternalOutput"
    T["z"] = nc.dram_tensor("z", [D, NT], F32, kind=ko)
    T["xn1b"] = nc.dram_tensor("xn1b", [D, NT], BF16, kind=ko)
    T["cwm"] = nc.dram_tensor("cwm", [NT, NEXP], F32, kind=ko)
    return T


def emit_l1a(nc, p, T, ctx, pre="l1a_"):
    x1_h = T["x1"]; W_h = T["W"]; proj_h = T["proj"]; ccT_h = T["ccT"]
    cb_h = T["cb"]; wsup_h = T["wsup"]; wsdn_h = T["wsdn"]; cent_h = T["cent"]
    bias_bc_h = T["bias_bc"]; onec_h = T["onec"]; ident_h = T["ident"]
    z_h = T["z"]; xn1b_h = T["xn1b"]; cwm_h = T["cwm"]

    sb = lambda name, shape, dt: ctx.enter_context(
        nc.sbuf_tensor(pre + name, shape, dt))
    psb = lambda name, shape, dt=F32: ctx.enter_context(
        nc.psum_tensor(pre + name, shape, dt))

    Wb = sb("Wb", [128, NCH * D], BF16)
    projb = sb("projb", [128, NCH], BF16)
    ccT = sb("ccTb", [128, 16 * D], BF16)
    cb = sb("cb_s", [128, NCH], F32)
    wsup = sb("wsup_b", [128, NCH * FEXP], BF16)
    wsdn = sb("wsdn_b", [128, 4 * D], BF16)
    cent = sb("cent_b", [128, NCH * NEXP], F32)
    bias_bc = sb("bias_bc_s", [128, NEXP], F32)
    onec = sb("onec_s", [128, 1], F32)

    xsl = [sb(f"xsl{i}", [128, NCH * TW], F32) for i in range(2)]
    xn1f = sb("xn1f", [128, NCH * TW], F32)
    xnb = sb("xnb", [128, NCH * TW], BF16)
    hsh = sb("hsh", [128, 4 * TW], BF16)
    sq = sb("sq", [128, 2 * TW], BF16)
    onecb = sb("onecb", [128, 1], BF16)
    mean = sb("mean", [1, TW], F32)
    msq = sb("msq", [1, TW], F32)
    var = sb("var", [1, TW], F32)
    varl = var
    rstd = msq
    mrs = mean
    rstd_bc = sb("rstd_bc", [128, TW], F32)
    mrs_bc = sb("mrs_bc", [128, TW], F32)
    aff = sb("aff", [128, NEXP], F32)
    top8 = sb("top8", [128, 8], F32)
    selm = sb("selm", [128, NEXP], F32)
    sgm = sb("sgm", [128, NEXP], F32)
    wsum = sb("wsum", [128, 1], F32)
    cwt = sb("cwt", [128, NEXP], F32)
    xpc = [sb(f"xpc{i}", [128, NCH * NB * 128], BF16) for i in range(2)]
    hpb = [sb(f"hpb{i}", [128, NCH * 128], BF16) for i in range(2)]
    scs = [sb(f"scs{i}", [128, NB], F32) for i in range(2)]
    tmpc = sb("tmpc", [128, NCH * 128], F32)
    emax = sb("emax", [128, 1], F32)
    gex = sb("gex", [128, 7], F32)
    cums = sb("cums", [128, 6], F32)
    rnv = sb("rnv", [128, 6], F32)
    trin = sb("trin", [128, 16], F32)
    trT = sb("trT", [16, 128], F32)
    gbc = sb("gbc", [128, 7 * 128], BF16)
    rnbc = sb("rnbc", [128, 6 * 128], F32)
    ctxa = sb("ctxa", [128, NCH * 128], F32)
    ctxb = sb("ctxb", [128, NCH * 6 * 128], BF16)
    attt = sb("attt", [128, NCH * 128], F32)
    ident = sb("ident_s", [128, 128], F32)
    tmp2 = sb("tmp2", [128, 128], F32)

    ps_su = psb("ps_su", [1, TW])
    ps_sq = psb("ps_sq", [1, TW])
    ps_m0 = psb("ps_m0", [128, TW])
    ps_m1 = psb("ps_m1", [128, TW])
    ps_aff = psb("ps_aff", [128, NEXP])
    ps_sc = [psb(f"ps_sc{i}", [128, NB]) for i in range(2)]
    ps_tr = psb("ps_tr", [16, 128])

    B = lambda n: Buf(n)
    b_w = {n: B(n) for n in ["W", "proj", "ccT", "cb", "wsup", "wsdn",
                             "cent", "bias_bc", "onec", "onecb", "ident"]}
    b_xsl = [[B(f"xsl{i}s{k}") for k in range(4)] for i in range(2)]
    b_xn1f = [B("xn1fa"), B("xn1fb")]
    b_xnb = [B("xnbh0"), B("xnbh1")]
    b_hsh = B("hsh")
    b_sq = [B(f"sq{i}") for i in range(2)]
    b_stat = B("stat")
    b_bc = B("bc")
    b_aff = B("aff")
    b_xpc = [B(f"xpc{i}") for i in range(2)]
    b_hpb = [B(f"hpb{i}") for i in range(2)]
    b_scs = [B(f"scs{i}") for i in range(2)]
    b_tmpc = B("tmpc")
    b_soft = B("soft")
    b_trT = B("trT")
    b_gbc = B("gbc")
    b_rnbc = B("rnbc")
    b_ctxa = B("ctxa")
    b_ctxb = B("ctxb")
    b_attt = B("attt")
    b_ps_su = B("ps_su")
    b_ps_sq = B("ps_sq")
    b_ps_m = [B("ps_m0"), B("ps_m1")]
    b_ps_aff = B("ps_aff")
    b_ps_sc = [B(f"ps_sc{i}") for i in range(2)]
    b_ps_tr = B("ps_tr")
    b_z = [[B(f"z{j}s{k}") for k in range(4)] for j in range(NTILE)]
    b_xn1b = [[B(f"xn1b{j}h{k}") for k in range(2)] for j in range(NTILE)]
    b_cwm = [B(f"cwm{j}") for j in range(NTILE)]

    p.dma("g", lambda E: E.dma_start(
        out=ap(Wb, 0, [[NCH * D, 128], [D, NCH], [1, D]]),
        in_=ap(W_h, 0, [[D, 128], [128 * D, NCH], [1, D]])),
        writes=[b_w["W"]])
    p.dma("g", lambda E: E.dma_start(
        out=ap(ccT, 0, [[16 * D, 128], [D, 16], [1, D]]),
        in_=ap(ccT_h, 0, [[D, 128], [128 * D, 16], [1, D]])),
        writes=[b_w["ccT"]])
    p.dma("g", lambda E: E.dma_start(
        out=ap(wsup, 0, [[NCH * FEXP, 128], [FEXP, NCH], [1, FEXP]]),
        in_=ap(wsup_h, 0, [[FEXP, 128], [128 * FEXP, NCH], [1, FEXP]])),
        writes=[b_w["wsup"]])
    p.dma("g", lambda E: E.dma_start(
        out=ap(wsdn, 0, [[4 * D, 128], [D, 4], [1, D]]),
        in_=ap(wsdn_h, 0, [[D, 128], [128 * D, 4], [1, D]])),
        writes=[b_w["wsdn"]])
    p.dma("q", lambda E: E.dma_start(
        out=ap(cent, 0, [[NCH * NEXP, 128], [NEXP, NCH], [1, NEXP]]),
        in_=ap(cent_h, 0, [[NEXP, 128], [128 * NEXP, NCH], [1, NEXP]])),
        writes=[b_w["cent"]])
    p.dma("g", lambda E: E.dma_start(out=projb[:, :], in_=proj_h[:, :]),
          writes=[b_w["proj"]])
    p.dma("g", lambda E: E.dma_start(out=onecb[:, :], in_=onec_h[:, :]),
          writes=[b_w["onecb"]])
    for hb, sbuf, key in [(cb_h, cb, "cb"), (bias_bc_h, bias_bc, "bias_bc"),
                          (onec_h, onec, "onec"), (ident_h, ident, "ident")]:
        p.dma("q", lambda E, hb=hb, sbuf=sbuf: E.dma_start(
            out=sbuf[:, :], in_=hb[:, :]), writes=[b_w[key]])

    def p1_load(j):
        par = j % 2
        js = j * TW
        for k in range(4):
            p.dma("q", lambda E, js=js, par=par, k=k: E.dma_start(
                out=ap(xsl[par], 2 * k * TW, [[NCH * TW, 128], [1, 2 * TW]]),
                in_=ap(x1_h, js * NCH + 2 * k * TW,
                       [[NTILE * NCH * TW, 128], [1, 2 * TW]])),
                writes=[b_xsl[par][k]])

    def p1_stats(j):
        par = j % 2
        xs, bxs = xsl[par], b_xsl[par]
        for ci in range(NCH):
            fn = lambda E, ci=ci, xs=xs: E.matmul(
                ps_su[0:1, :], onec[:, 0:1], xs[:, ci * TW:(ci + 1) * TW],
                start=(ci == 0), stop=(ci == NCH - 1))
            if ci < NCH - 1:
                p.gop("tensor", fn, reads=[b_w["onec"], bxs[ci // 2]], writes=[b_ps_su])
            else:
                p.op("tensor", fn, reads=[b_w["onec"], bxs[ci // 2]], writes=[b_ps_su])
        for ci in range(NCH):
            sl = ci % 2
            p.op("scalar", lambda E, ci=ci, sl=sl, xs=xs: E.activation(
                sq[:, sl * TW:(sl + 1) * TW], xs[:, ci * TW:(ci + 1) * TW],
                AF.Square), reads=[bxs[ci // 2]], writes=[b_sq[sl]])
            p.op("tensor", lambda E, ci=ci, sl=sl: E.matmul(
                ps_sq[0:1, :], onecb[:, 0:1], sq[:, sl * TW:(sl + 1) * TW],
                start=(ci == 0), stop=(ci == NCH - 1)),
                reads=[b_w["onecb"], b_sq[sl]], writes=[b_ps_sq])
        p.op("scalar", lambda E: E.activation(mean[0:1, :], ps_su[0:1, :],
                                              AF.Copy, scale=1.0 / D),
             reads=[b_ps_su, p.cst(1.0 / D)], writes=[b_stat])
        p.op("scalar", lambda E: E.activation(msq[0:1, :], ps_sq[0:1, :],
                                              AF.Copy, scale=1.0 / D),
             reads=[b_ps_sq, p.cst(1.0 / D)], writes=[b_stat])
        p.op("vector", lambda E: E.tensor_mul(var[0:1, :], mean[0:1, :],
                                              mean[0:1, :]), rw=[b_stat])
        p.op("vector", lambda E: E.tensor_sub(var[0:1, :], msq[0:1, :],
                                              var[0:1, :]), rw=[b_stat])
        p.op("scalar", lambda E: E.activation(varl[0:1, :], var[0:1, :],
                                              AF.Ln, bias=1e-5),
             reads=[p.cst(1e-5)], rw=[b_stat])
        p.op("scalar", lambda E: E.activation(rstd[0:1, :], varl[0:1, :],
                                              AF.Exp, scale=-0.5),
             reads=[p.cst(-0.5)], rw=[b_stat])
        p.op("vector", lambda E: E.tensor_mul(mrs[0:1, :], mean[0:1, :],
                                              rstd[0:1, :]), rw=[b_stat])
        p.dma("q", lambda E: E.dma_start(
            out=rstd_bc[:, :], in_=ap(rstd, 0, [[TW, 1], [0, 128], [1, TW]])),
            reads=[b_stat], writes=[b_bc])
        p.dma("q", lambda E: E.dma_start(
            out=mrs_bc[:, :], in_=ap(mrs, 0, [[TW, 1], [0, 128], [1, TW]])),
            reads=[b_stat], writes=[b_bc])

    def p1_apply(j):
        par = j % 2
        js = j * TW
        xs, bxs = xsl[par], b_xsl[par]
        xb, bxb = xnb, b_xnb
        for ci in range(NCH):
            half = ci // 4
            eng = "gpsimd" if half == 0 else "vector"
            bx = b_xn1f[half]
            p.op(eng, lambda E, ci=ci, xs=xs: E.tensor_mul(
                xn1f[:, ci * TW:(ci + 1) * TW], xs[:, ci * TW:(ci + 1) * TW],
                rstd_bc[:, :]), reads=[bxs[ci // 2], b_bc], writes=[bx])
            p.op(eng, lambda E, ci=ci: E.tensor_sub(
                xn1f[:, ci * TW:(ci + 1) * TW], xn1f[:, ci * TW:(ci + 1) * TW],
                mrs_bc[:, :]), reads=[b_bc], rw=[bx])
            p.op("scalar", lambda E, ci=ci, xb=xb: E.activation(
                xb[:, ci * TW:(ci + 1) * TW], xn1f[:, ci * TW:(ci + 1) * TW],
                AF.Copy), reads=[bx], writes=[bxb[ci // 4]])
        for h in range(2):
            p.dma("g", lambda E, js=js, xb=xb, h=h: E.dma_start(
                out=ap(xn1b_h, js + 4 * h * 128 * NT,
                       [[NT, 128], [128 * NT, 4], [1, TW]]),
                in_=ap(xb, 4 * h * TW, [[NCH * TW, 128], [TW, 4], [1, TW]])),
                reads=[bxb[h]], writes=[b_xn1b[j][h]])
        if j < 2:
            for ci in range(NCH):
                p.op("vector", lambda E, ci=ci, xs=xs: E.tensor_add(
                    xs[:, ci * TW:(ci + 1) * TW], xs[:, ci * TW:(ci + 1) * TW],
                    xn1f[:, ci * TW:(ci + 1) * TW]),
                    reads=[b_xn1f[ci // 4]], rw=[bxs[ci // 2]])

    def p1_routing(j):
        js = j * TW
        for st in range(4):
            so = st * 128
            for ci in range(NCH):
                fn = lambda E, ci=ci, so=so: E.matmul(
                    ps_aff[:, :], xn1f[:, ci * TW + so:ci * TW + so + 128],
                    cent[:, ci * NEXP:(ci + 1) * NEXP],
                    start=(ci == 0), stop=(ci == NCH - 1))
                if ci < NCH - 1:
                    p.gop("tensor", fn, reads=[b_xn1f[ci // 4], b_w["cent"]],
                          writes=[b_ps_aff])
                else:
                    p.op("tensor", fn, reads=[b_xn1f[ci // 4], b_w["cent"]],
                         writes=[b_ps_aff])
            p.op("vector", lambda E: E.tensor_add(
                aff[:, :], ps_aff[:, :], bias_bc[:, :]),
                reads=[b_ps_aff, b_w["bias_bc"]], writes=[b_aff])
            p.op("vector", lambda E: E.max(top8[:, :], aff[:, :]),
                 reads=[b_aff], writes=[b_aff])
            p.op("vector", lambda E: E.tensor_scalar(
                selm[:, :], aff[:, :], top8[:, 7:8], None, ALU.is_ge),
                reads=[b_aff], writes=[b_aff])
            p.op("vector", lambda E: E.tensor_sub(
                aff[:, :], aff[:, :], bias_bc[:, :]),
                reads=[b_w["bias_bc"]], rw=[b_aff])
            p.op("scalar", lambda E: E.activation(sgm[:, :], aff[:, :],
                                                  AF.Sigmoid),
                 reads=[b_aff], writes=[b_aff])
            p.op("vector", lambda E: E.tensor_mul(sgm[:, :], sgm[:, :],
                                                  selm[:, :]), rw=[b_aff])
            p.op("vector", lambda E: E.reduce_sum(wsum[:, :], sgm[:, :],
                                                  axis=AXX), rw=[b_aff])
            p.op("vector", lambda E: E.tensor_scalar(
                wsum[:, :], wsum[:, :], 1e-8, None, ALU.add), rw=[b_aff])
            p.op("vector", lambda E: E.reciprocal(wsum[:, :], wsum[:, :]),
                 rw=[b_aff])
            p.op("vector", lambda E: E.tensor_scalar(
                cwt[:, :], sgm[:, :], wsum[:, 0:1], None, ALU.mult),
                rw=[b_aff])
            p.dma("a", lambda E, js=js, so=so: E.dma_start(
                out=cwm_h[js + so:js + so + 128, :], in_=cwt[:, :]),
                reads=[b_aff], writes=[b_cwm[j]])

    def p1_ffn(j):
        par = j % 2
        js = j * TW
        xs, bxs = xsl[par], b_xsl[par]
        xb, bxb = xnb, b_xnb
        for fi in range(4):
            pm, bpm = (ps_m0, b_ps_m[0]) if fi % 2 == 0 else (ps_m1, b_ps_m[1])
            for ci in range(NCH):
                fn = lambda E, ci=ci, fi=fi, pm=pm, xb=xb: E.matmul(
                    pm[:, :], wsup[:, ci * FEXP + fi * 128:ci * FEXP + (fi + 1) * 128],
                    xb[:, ci * TW:(ci + 1) * TW],
                    start=(ci == 0), stop=(ci == NCH - 1))
                if ci < NCH - 1:
                    p.gop("tensor", fn, reads=[b_w["wsup"], bxb[ci // 4]], writes=[bpm])
                else:
                    p.op("tensor", fn, reads=[b_w["wsup"], bxb[ci // 4]], writes=[bpm])
            p.op("scalar", lambda E, fi=fi, pm=pm: E.activation(
                hsh[:, fi * TW:(fi + 1) * TW], pm[:, :], AF.Silu),
                reads=[bpm], writes=[b_hsh])
        for dj in range(NCH):
            pm, bpm = (ps_m0, b_ps_m[0]) if dj % 2 == 0 else (ps_m1, b_ps_m[1])
            for fi in range(4):
                fn = lambda E, fi=fi, dj=dj, pm=pm: E.matmul(
                    pm[:, :], wsdn[:, fi * D + dj * 128:fi * D + (dj + 1) * 128],
                    hsh[:, fi * TW:(fi + 1) * TW],
                    start=(fi == 0), stop=(fi == 3))
                if fi < 3:
                    p.gop("tensor", fn, reads=[b_w["wsdn"], b_hsh], writes=[bpm])
                else:
                    p.op("tensor", fn, reads=[b_w["wsdn"], b_hsh], writes=[bpm])
            p.op("vector", lambda E, dj=dj, pm=pm, xs=xs: E.scalar_tensor_tensor(
                xs[:, dj * TW:(dj + 1) * TW], pm[:, :], 1.0,
                xs[:, dj * TW:(dj + 1) * TW], ALU.mult, ALU.add),
                reads=[bpm], rw=[bxs[dj // 2]])
        for k, qq in enumerate(("a", "a", "g", "g")):
            p.dma(qq, lambda E, js=js, xs=xs, k=k: E.dma_start(
                out=ap(z_h, js + 2 * k * 128 * NT,
                       [[NT, 128], [128 * NT, 2], [1, TW]]),
                in_=ap(xs, 2 * k * TW, [[NCH * TW, 128], [TW, 2], [1, TW]])),
                reads=[bxs[k]], writes=[b_z[j][k]])

    p1_load(0)
    p1_stats(0)
    p1_apply(0)
    for j in range(NTILE):
        if j + 1 < NTILE:
            p1_load(j + 1)
        p1_ffn(j)
        if j + 1 < NTILE:
            p1_stats(j + 1)
        p1_routing(j)
        if j + 1 < NTILE:
            p1_apply(j + 1)

    def p2_stage1(pc):
        par = pc % 2
        po = pc * 128
        xq, bxq = xpc[par], b_xpc[par]
        hq, bhq = hpb[par], b_hpb[par]
        sq_, bsq = scs[par], b_scs[par]
        pssc, bpssc = ps_sc[par], b_ps_sc[par]
        for ci in range(NCH):
            p.dma("q", lambda E, po=po, ci=ci, xq=xq: E.dma_start(
                out=ap(xq, ci * NB * 128, [[NCH * NB * 128, 128], [128, NB], [1, 128]]),
                in_=ap(xn1b_h, ci * 128 * NT + po, [[NT, 128], [PL, NB], [1, 128]])),
                reads=[b for t in range(NTILE) for b in b_xn1b[t]], writes=[bxq])
        for s in range(7):
            for ki in range(NCH):
                pm, bpm = (ps_m0, b_ps_m[0]) if ki % 2 == 0 else (ps_m1, b_ps_m[1])
                for ci in range(NCH):
                    fn = lambda E, ci=ci, ki=ki, s=s, pm=pm, xq=xq: E.matmul(
                        pm[:, 0:128],
                        Wb[:, ci * D + ki * 128:ci * D + (ki + 1) * 128],
                        xq[:, ci * NB * 128 + s * 128:ci * NB * 128 + (s + 1) * 128],
                        start=(ci == 0), stop=(ci == NCH - 1))
                    if ci < NCH - 1:
                        p.gop("tensor", fn, reads=[b_w["W"], bxq], writes=[bpm])
                    else:
                        p.op("tensor", fn, reads=[b_w["W"], bxq], writes=[bpm])
                p.op("scalar", lambda E, ki=ki, pm=pm, hq=hq: E.activation(
                    hq[:, ki * 128:(ki + 1) * 128], pm[:, 0:128], AF.Tanh),
                    reads=[bpm], writes=[bhq])
            for ki in range(NCH):
                fn = lambda E, ki=ki, s=s, hq=hq, pssc=pssc: E.matmul(
                    pssc[:, s:s + 1], hq[:, ki * 128:(ki + 1) * 128],
                    projb[:, ki:ki + 1], start=(ki == 0), stop=(ki == NCH - 1))
                if ki < NCH - 1:
                    p.gop("tensor", fn, reads=[bhq, b_w["proj"]], writes=[bpssc])
                else:
                    p.op("tensor", fn, reads=[bhq, b_w["proj"]], writes=[bpssc])
            p.op("vector", lambda E, s=s, sq_=sq_, pssc=pssc: E.tensor_copy(
                sq_[:, s:s + 1], pssc[:, s:s + 1]),
                reads=[bpssc], writes=[bsq])

    def p2_s2a(pc):
        par = pc % 2
        po = pc * 128
        xq, bxq = xpc[par], b_xpc[par]
        sq_, bsq = scs[par], b_scs[par]
        p.op("vector", lambda E, sq_=sq_: E.reduce_max(
            emax[:, :], sq_[:, 0:7], axis=AXX), reads=[bsq], writes=[b_soft])
        p.op("vector", lambda E: E.tensor_scalar(
            emax[:, :], emax[:, :], -1.0, None, ALU.mult), rw=[b_soft])
        p.op("scalar", lambda E, sq_=sq_: E.activation(
            gex[:, :], sq_[:, 0:7], AF.Exp, bias=emax[:, 0:1]),
            reads=[bsq], rw=[b_soft])
        p.op("vector", lambda E: E.tensor_add(cums[:, 0:1], gex[:, 0:1],
                                              gex[:, 1:2]), rw=[b_soft])
        for t in range(3, 8):
            p.op("vector", lambda E, t=t: E.tensor_add(
                cums[:, t - 2:t - 1], cums[:, t - 3:t - 2], gex[:, t - 1:t]),
                rw=[b_soft])
        p.op("vector", lambda E: E.reciprocal(rnv[:, :], cums[:, :]),
             rw=[b_soft])
        p.op("vector", lambda E: E.tensor_copy(trin[:, 0:7], gex[:, :]),
             reads=[b_soft], writes=[b_trT])
        p.op("vector", lambda E: E.tensor_copy(trin[:, 7:13], rnv[:, :]),
             reads=[b_soft], writes=[b_trT])
        p.op("vector", lambda E: E.memset(trin[:, 13:16], 0.0), writes=[b_trT])
        p.op("tensor", lambda E: E.transpose(ps_tr[:, :], trin[:, 0:16],
                                             ident[:, :]),
             reads=[b_trT, b_w["ident"]], writes=[b_ps_tr])
        p.op("vector", lambda E: E.tensor_copy(trT[:, :], ps_tr[:, :]),
             reads=[b_ps_tr], writes=[b_trT])
        for s in range(7):
            p.dma("g", lambda E, s=s: E.dma_start(
                out=gbc[:, s * 128:(s + 1) * 128],
                in_=ap(trT, s * 128, [[128, 1], [0, 128], [1, 128]])),
                reads=[b_trT], writes=[b_gbc])
        for t in range(6):
            p.dma("q", lambda E, t=t: E.dma_start(
                out=rnbc[:, t * 128:(t + 1) * 128],
                in_=ap(trT, (7 + t) * 128, [[128, 1], [0, 128], [1, 128]])),
                reads=[b_trT], writes=[b_rnbc])
    def p2_s2b(pc):
        par = pc % 2
        po = pc * 128
        xq, bxq = xpc[par], b_xpc[par]
        for s in range(7):
            xin = lambda E, s=s, xq=xq: ap(
                xq, s * 128, [[NCH * NB * 128, 128], [NB * 128, NCH], [1, 128]])
            gin = lambda E, s=s: ap(gbc, s * 128, [[7 * 128, 128], [0, NCH], [1, 128]])
            if s == 0:
                p.op("vector", lambda E, s=s, xq=xq: E.tensor_mul(
                    ctxa[:, :],
                    ap(xq, s * 128, [[NCH * NB * 128, 128], [NB * 128, NCH], [1, 128]]),
                    ap(gbc, s * 128, [[7 * 128, 128], [0, NCH], [1, 128]])),
                    reads=[bxq, b_gbc], writes=[b_ctxa])
            else:
                p.op("vector", lambda E, s=s, xq=xq: E.tensor_mul(
                    tmpc[:, :],
                    ap(xq, s * 128, [[NCH * NB * 128, 128], [NB * 128, NCH], [1, 128]]),
                    ap(gbc, s * 128, [[7 * 128, 128], [0, NCH], [1, 128]])),
                    reads=[bxq, b_gbc], writes=[b_tmpc])
                p.op("vector", lambda E: E.tensor_add(
                    ctxa[:, :], ctxa[:, :], tmpc[:, :]),
                    reads=[b_tmpc], rw=[b_ctxa])
            t = s + 1
            if 2 <= t <= 7:
                p.op("vector", lambda E, t=t: E.tensor_mul(
                    ap(ctxb, (t - 2) * 128,
                       [[NCH * 6 * 128, 128], [6 * 128, NCH], [1, 128]]),
                    ctxa[:, :],
                    ap(rnbc, (t - 2) * 128, [[6 * 128, 128], [0, NCH], [1, 128]])),
                    reads=[b_ctxa, b_rnbc], writes=[b_ctxb])
        for t in range(2, 8):
            for hj in range(NCH):
                pm, bpm = (ps_m0, b_ps_m[0]) if hj % 2 == 0 else (ps_m1, b_ps_m[1])
                for c2 in range(16):
                    if c2 < 8:
                        rhs = lambda E, c2=c2, t=t, xq=xq: xq[:, c2 * NB * 128 + t * 128:
                                                             c2 * NB * 128 + (t + 1) * 128]
                        rb = bxq
                    else:
                        rhs = lambda E, c2=c2, t=t: ctxb[:, ((c2 - 8) * 6 + t - 2) * 128:
                                                         ((c2 - 8) * 6 + t - 1) * 128]
                        rb = b_ctxb
                    fn = lambda E, c2=c2, hj=hj, rhs=rhs, pm=pm: E.matmul(
                        pm[:, 0:128],
                        ccT[:, c2 * D + hj * 128:c2 * D + (hj + 1) * 128],
                        rhs(E), start=(c2 == 0), stop=(c2 == 15))
                    if c2 < 15:
                        p.gop("tensor", fn, reads=[b_w["ccT"], rb], writes=[bpm])
                    else:
                        p.op("tensor", fn, reads=[b_w["ccT"], rb], writes=[bpm])
                p.op("scalar", lambda E, hj=hj, pm=pm: E.activation(
                    attt[:, hj * 128:(hj + 1) * 128], pm[:, 0:128], AF.Tanh,
                    bias=cb[:, hj:hj + 1]),
                    reads=[bpm, b_w["cb"]], writes=[b_attt])
            p.dma("g", lambda E, t=t, po=po: E.dma_start(
                out=ap(z_h, t * PL + po, [[NT, 128], [128 * NT, NCH], [1, 128]]),
                in_=ap(attt, 0, [[NCH * 128, 128], [128, NCH], [1, 128]]),
                accum_op=ALU.add),
                reads=[b_attt], rw=b_z[t])

    p2_stage1(0)
    for pc in range(4):
        p2_s2a(pc)
        if pc + 1 < 4:
            p2_stage1(pc + 1)
        p2_s2b(pc)

    out_tokens = []
    for j in range(NTILE):
        for b in (*b_z[j], *b_xn1b[j], b_cwm[j]):
            if b.writer:
                out_tokens.append(b.writer)
    best = {}
    for s, v in out_tokens:
        best[s] = max(best.get(s, 0), v)
    return sorted(best.items())


def build_l1a():
    nc = bass.Bass("TRN2", target_bir_lowering=False, debug=False)
    ctx = ExitStack()
    T = {"x1": nc.dram_tensor("x1", [128, NTILE * NCH * TW], F32, kind="ExternalInput"),
         "onec": nc.dram_tensor("onec", [128, 1], F32, kind="ExternalInput")}
    _decl_l1a_drams(nc, fused=False, T=T)
    p = Prog(nc, ctx)
    toks = emit_l1a(nc, p, T, ctx)
    p.finalize(toks)
    ctx.close()
    return nc



NGRP = 2
GW = NT // NGRP
GTILES = GW // TW

def _decl_l1b_drams(nc, T=None):
    if T is None:
        T = {}
    T["wup"] = nc.dram_tensor("wup", [NEXP * D, FEXP], F32, kind="ExternalInput")
    T["wdn"] = nc.dram_tensor("wdn", [NEXP * FEXP, D], F32, kind="ExternalInput")
    T["out"] = nc.dram_tensor("out", [D, NT], F32, kind="ExternalOutput")
    T["cwTr"] = nc.dram_tensor("cwTr", [NEXP, NT], F32, kind="Internal")
    return T


def emit_l1b(nc, p, T, ctx, pre="l1b_"):
    z_h = T["z"]; xn1b_h = T["xn1b"]; cwm_h = T["cwm"]; wup_h = T["wup"]
    wdn_h = T["wdn"]; ident_h = T["ident"]; out_h = T["out"]; cwT_h = T["cwTr"]

    sb = lambda name, shape, dt: ctx.enter_context(
        nc.sbuf_tensor(pre + name, shape, dt))
    psb = lambda name, shape, dt=F32: ctx.enter_context(
        nc.psum_tensor(pre + name, shape, dt))

    NSLOT = 4
    wu = [sb(f"wu{k}", [128, NCH * FEXP], BF16) for k in range(NSLOT)]
    wd = [sb(f"wd{k}", [128, 4 * D], BF16) for k in range(NSLOT)]
    xng = sb("xng", [128, NCH * GW], BF16)
    accg = sb("accg", [128, NCH * GW], F32)
    hf = [[sb(f"hf{pr}_{k}", [128, 4 * TW], BF16) for k in range(2)]
          for pr in range(2)]
    htmp = sb("htmp", [128, TW], F32)
    cwbc = [sb(f"cwbc{k}", [128, TW], F32) for k in range(2)]
    zsl = sb("zsl", [128, NCH * TW], F32)
    cwtile = sb("cwtile", [128, NEXP], F32)
    cwev = sb("cwev", [NEXP, 128], F32)
    ident = sb("ident_s", [128, 128], F32)

    ps_u0 = psb("ps_u0", [128, TW])
    ps_u1 = psb("ps_u1", [128, TW])
    ps_d0 = psb("ps_d0", [128, TW])
    ps_d1 = psb("ps_d1", [128, TW])
    ps_t = psb("ps_t", [NEXP, 128])

    B = lambda n: Buf(n)
    b_ident = B("ident")
    b_wu = [B(f"wu{k}") for k in range(NSLOT)]
    b_wd = [B(f"wd{k}") for k in range(NSLOT)]
    b_xng = B("xng")
    b_accg = B("accg")
    b_hf = [[B(f"hf{pr}_{k}") for k in range(2)] for pr in range(2)]
    b_htmp = B("htmp")
    b_cwbc = [B(f"cwbc{k}") for k in range(2)]
    b_zsl = B("zsl")
    b_cwtile = B("cwtile")
    b_cwev = B("cwev")
    b_psu = [B("ps_u0"), B("ps_u1")]
    b_psd = [B("ps_d0"), B("ps_d1")]
    b_pst = B("ps_t")
    b_cwT = B("cwT_h")
    b_out = [B(f"out{j}") for j in range(NTILE)]

    p.dma("q", lambda E: E.dma_start(out=ident[:, :], in_=ident_h[:, :]),
          writes=[b_ident])

    for st in range(NT // 128):
        p.dma("q", lambda E, st=st: E.dma_start(
            out=cwtile[:, :], in_=cwm_h[st * 128:(st + 1) * 128, :]),
            writes=[b_cwtile])
        p.op("tensor", lambda E: E.transpose(ps_t[:, :], cwtile[:, :],
                                             ident[:, :]),
             reads=[b_cwtile, b_ident], writes=[b_pst])
        p.op("vector", lambda E: E.tensor_copy(cwev[:, :], ps_t[:, :]),
             reads=[b_pst], writes=[b_cwev])
        p.dma("a", lambda E, st=st: E.dma_start(
            out=ap(cwT_h, st * 128, [[NT, NEXP], [1, 128]]),
            in_=cwev[:, :]), reads=[b_cwev], writes=[b_cwT])

    def load_expert(e):
        k = e % NSLOT
        p.dma("g", lambda E, e=e, k=k: E.dma_start(
            out=ap(wu[k], 0, [[NCH * FEXP, 128], [FEXP, NCH], [1, FEXP]]),
            in_=ap(wup_h, e * D * FEXP,
                   [[FEXP, 128], [128 * FEXP, NCH], [1, FEXP]])),
            writes=[b_wu[k]])
        p.dma("g", lambda E, e=e, k=k: E.dma_start(
            out=ap(wd[k], 0, [[4 * D, 128], [D, 4], [1, D]]),
            in_=ap(wdn_h, e * FEXP * D, [[D, 128], [128 * D, 4], [1, D]])),
            writes=[b_wd[k]])

    def emit_final(g):
        go = g * GW
        for tile in range(GTILES):
            ts = go + tile * TW
            j = ts // TW
            p.dma("q", lambda E, ts=ts: E.dma_start(
                out=ap(zsl, 0, [[NCH * TW, 128], [TW, NCH], [1, TW]]),
                in_=ap(z_h, ts, [[NT, 128], [128 * NT, NCH], [1, TW]])),
                writes=[b_zsl])
            for dj in range(NCH):
                p.op("vector", lambda E, dj=dj, tile=tile: E.tensor_add(
                    zsl[:, dj * TW:(dj + 1) * TW],
                    zsl[:, dj * TW:(dj + 1) * TW],
                    accg[:, dj * GW + tile * TW:dj * GW + tile * TW + TW]),
                    reads=[b_accg], rw=[b_zsl])
            p.dma("a", lambda E, ts=ts: E.dma_start(
                out=ap(out_h, ts, [[NT, 128], [128 * NT, NCH], [1, TW]]),
                in_=ap(zsl, 0, [[NCH * TW, 128], [TW, NCH], [1, TW]])),
                reads=[b_zsl], writes=[b_out[j]])

    b_xngc = [Buf(f"xng{ci}") for ci in range(NCH)]
    npairs = NEXP // 2
    seq = [(g, pair) for g in range(NGRP) for pair in range(npairs)]
    for e in range(4):
        load_expert(e)

    def emit_down(dn):
        (e0, k0, e1, k1, par, tile, g, first) = dn
        go = g * GW
        for dj in range(NCH):
            pm, bpm = (ps_d0, b_psd[0]) if dj % 2 == 0 else (ps_d1, b_psd[1])
            nmm = 0
            for (e, k, hx) in [(e0, k0, 0), (e1, k1, 1)]:
                for fi in range(4):
                    nmm += 1
                    fn = lambda E, fi=fi, dj=dj, k=k, hx=hx, par=par, pm=pm, nmm=nmm: E.matmul(
                        pm[:, :],
                        wd[k][:, fi * D + dj * 128:fi * D + (dj + 1) * 128],
                        hf[par][hx][:, fi * TW:(fi + 1) * TW],
                        start=(nmm == 1), stop=(nmm == 8))
                    if nmm < 8:
                        p.gop("tensor", fn, reads=[b_wd[k], b_hf[par][hx]],
                              writes=[bpm])
                    else:
                        p.op("tensor", fn, reads=[b_wd[k], b_hf[par][hx]],
                             writes=[bpm])
            if first:
                p.op("vector", lambda E, dj=dj, tile=tile, go=go, pm=pm: E.tensor_copy(
                    accg[:, dj * GW + tile * TW:dj * GW + tile * TW + TW],
                    pm[:, :]), reads=[bpm], writes=[b_accg])
            else:
                p.op("vector", lambda E, dj=dj, tile=tile, go=go, pm=pm: E.scalar_tensor_tensor(
                    accg[:, dj * GW + tile * TW:dj * GW + tile * TW + TW],
                    pm[:, :], 1.0,
                    accg[:, dj * GW + tile * TW:dj * GW + tile * TW + TW],
                    ALU.mult, ALU.add), reads=[bpm], rw=[b_accg])

    pending = None
    upcnt = 0
    for qi, (g, pair) in enumerate(seq):
        go = g * GW
        if pair == 0:
            if pending is not None:
                emit_down(pending)
                pending = None
            if g > 0:
                emit_final(g - 1)
            for ci in range(NCH):
                p.dma("q", lambda E, ci=ci, go=go: E.dma_start(
                    out=ap(xng, ci * GW, [[NCH * GW, 128], [1, GW]]),
                    in_=ap(xn1b_h, ci * 128 * NT + go, [[NT, 128], [1, GW]])),
                    writes=[b_xngc[ci]])
        e0, e1 = 2 * pair, 2 * pair + 1
        k0, k1 = e0 % NSLOT, e1 % NSLOT
        for tile in range(GTILES):
            ts = go + tile * TW
            par = upcnt % 2
            upcnt += 1
            for (e, k, hx) in [(e0, k0, 0), (e1, k1, 1)]:
                p.dma("q", lambda E, e=e, ts=ts, hx=hx: E.dma_start(
                    out=cwbc[hx][:, :],
                    in_=ap(cwT_h, e * NT + ts, [[NT, 1], [0, 128], [1, TW]])),
                    reads=[b_cwT], writes=[b_cwbc[hx]])
                for fi in range(4):
                    pm, bpm = (ps_u0, b_psu[0]) if fi % 2 == 0 else (ps_u1, b_psu[1])
                    for ci in range(NCH):
                        fn = lambda E, ci=ci, fi=fi, k=k, tile=tile, go=go, pm=pm: E.matmul(
                            pm[:, :],
                            wu[k][:, ci * FEXP + fi * 128:ci * FEXP + (fi + 1) * 128],
                            xng[:, ci * GW + tile * TW:ci * GW + tile * TW + TW],
                            start=(ci == 0), stop=(ci == NCH - 1))
                        if ci < NCH - 1:
                            p.gop("tensor", fn, reads=[b_wu[k], b_xngc[ci]],
                                  writes=[bpm])
                        else:
                            p.op("tensor", fn, reads=[b_wu[k], b_xngc[ci]],
                                 writes=[bpm])
                    p.op("scalar", lambda E, pm=pm: E.activation(
                        htmp[:, :], pm[:, :], AF.Silu),
                        reads=[bpm], writes=[b_htmp])
                    p.op("vector", lambda E, fi=fi, hx=hx, par=par: E.tensor_mul(
                        hf[par][hx][:, fi * TW:(fi + 1) * TW], htmp[:, :],
                        cwbc[hx][:, :]),
                        reads=[b_htmp, b_cwbc[hx]], writes=[b_hf[par][hx]])
            if pending is not None:
                emit_down(pending)
            if tile == 0 and qi >= 1 and qi + 1 < len(seq):
                npair = seq[qi + 1][1]
                load_expert(2 * npair)
                load_expert(2 * npair + 1)
            pending = (e0, k0, e1, k1, par, tile, g, pair == 0)
    emit_down(pending)
    emit_final(NGRP - 1)

    best = {}
    for j in range(NTILE):
        if b_out[j].writer:
            s, v = b_out[j].writer
            best[s] = max(best.get(s, 0), v)
    return sorted(best.items())


def build_l1b():
    nc = bass.Bass("TRN2", target_bir_lowering=False, debug=False)
    ctx = ExitStack()
    T = {"z": nc.dram_tensor("z", [D, NT], F32, kind="ExternalInput"),
         "xn1b": nc.dram_tensor("xn1b", [D, NT], BF16, kind="ExternalInput"),
         "cwm": nc.dram_tensor("cwm", [NT, NEXP], F32, kind="ExternalInput"),
         "ident": nc.dram_tensor("ident", [128, 128], F32, kind="ExternalInput")}
    _decl_l1b_drams(nc, T=T)
    p = Prog(nc, ctx)
    toks = emit_l1b(nc, p, T, ctx)
    p.finalize(toks)
    ctx.close()
    return nc


def build_fused():
    nc = bass.Bass("TRN2", target_bir_lowering=False, debug=False)
    outer = ExitStack()
    T = _decl_l0_drams(nc, fused=True)
    _decl_l1a_drams(nc, fused=True, T=T)
    _decl_l1b_drams(nc, T=T)
    p = Prog(nc, outer)
    for v in (1.0 / D, 1e-5, -0.5, 512.0):
        p.cst(v)
    with ExitStack() as ph:
        emit_l0(nc, p, T, ph)
    p.barrier()
    with ExitStack() as ph:
        emit_l1a(nc, p, T, ph)
    p.barrier()
    with ExitStack() as ph:
        toks = emit_l1b(nc, p, T, ph)
    p.finalize(toks)
    outer.close()
    return nc



def _f(a):
    return np.ascontiguousarray(a, dtype=np.float32)


def prep_l0_inputs(inputs):
    x = np.asarray(inputs["L0_attn_x"])
    B_w = np.asarray(inputs["L0_attn_B_w"])
    C_w = np.asarray(inputs["L0_attn_C_w"])
    logA = np.asarray(inputs["L0_attn_log_A"])
    dpar = np.asarray(inputs["L0_attn_D_param"])
    W_r = np.asarray(inputs["L0_ffn_W_r"])
    W_k = np.asarray(inputs["L0_ffn_W_k"])
    W_v = np.asarray(inputs["L0_ffn_W_v"])
    mur = np.asarray(inputs["L0_ffn_mu_r"])
    muk = np.asarray(inputs["L0_ffn_mu_k"])

    shared = dict(
        bwT=_f(B_w.T),
        cwT=_f(C_w.T),
        wrT=_f(W_r.T), wkT=_f(W_k.T), wvT=_f(W_v.T),
        logA=_f(logA.reshape(128, 1)),
        dpar=_f(dpar.reshape(NCH, 128).T),
        mur=_f(mur.reshape(NCH, 128).T),
        muk=_f(muk.reshape(NCH, 128).T),
        onec=_f(np.ones((128, 1))),
    )
    in_maps = []
    for c in range(NCORE):
        m = dict(shared)
        xs = x[:, c * PL:(c + 1) * PL, :]
        xfm = xs.transpose(2, 0, 1).reshape(D, NT)
        m["x"] = _f(xfm.reshape(NCH, 128, NTILE, TW).transpose(1, 2, 0, 3)
                    .reshape(128, NTILE * NCH * TW))
        if c == 0:
            halo = np.zeros((NB, D), dtype=np.float32)
        else:
            halo = _f(x[:, c * PL - 1, :])
        m["halo"] = _f(halo.T.reshape(NCH, 128, NB).transpose(1, 0, 2)
                       .reshape(128, NCH * NB))
        hp = np.zeros((128, NCORE * NB), dtype=np.float32)
        if c > 0:
            for b in range(NB):
                hp[:, (c - 1) * NB + b] = 1.0
        m["hpick"] = hp
        in_maps.append(m)
    return in_maps


def prep_l1a_inputs(inputs, x1_fm_percore):
    W = np.asarray(inputs["L1_attn_W"])
    proj = np.asarray(inputs["L1_attn_proj"])
    ccw = np.asarray(inputs["L1_attn_concat_w"])
    cbv = np.asarray(inputs["L1_attn_concat_b"])
    cent = np.asarray(inputs["L1_ffn_centroids"])
    ebias = np.asarray(inputs["L1_ffn_expert_biases"])
    wsup = np.asarray(inputs["L1_ffn_Ws_up"])
    wsdn = np.asarray(inputs["L1_ffn_Ws_down"])
    shared = dict(
        W=_f(W),
        proj=_f(proj.reshape(NCH, 128).T),
        ccT=_f(ccw.T),
        cb=_f(cbv.reshape(NCH, 128).T),
        wsup=_f(wsup),
        wsdn=_f(wsdn),
        cent=_f(cent.T),
        bias_bc=_f(np.tile(ebias, (128, 1))),
        onec=_f(np.ones((128, 1))),
        ident=_f(np.eye(128)),
    )
    in_maps = []
    for c in range(NCORE):
        m = dict(shared)
        m["x1"] = _f(x1_fm_percore[c])
        in_maps.append(m)
    return in_maps


def prep_l1b_inputs(inputs, res1_list):
    wup = np.asarray(inputs["L1_ffn_W_up"])
    wdn = np.asarray(inputs["L1_ffn_W_down"])
    shared = dict(
        wup=_f(wup.reshape(NEXP * D, FEXP)),
        wdn=_f(wdn.reshape(NEXP * FEXP, D)),
        ident=_f(np.eye(128)),
    )
    in_maps = []
    for c in range(NCORE):
        m = dict(shared)
        m["z"] = np.asarray(res1_list[c]["z"])
        m["xn1b"] = np.asarray(res1_list[c]["xn1b"])
        m["cwm"] = np.asarray(res1_list[c]["cwm"])
        in_maps.append(m)
    return in_maps


def assemble_fm(res_list, key):
    out = np.empty((NB, NCORE * PL, D), dtype=np.float32)
    for c, r in enumerate(res_list):
        a = np.asarray(r[key])
        out[:, c * PL:(c + 1) * PL, :] = (
            a.reshape(D, NB, PL).transpose(1, 2, 0))
    return out


def prep_fused_inputs(inputs):
    maps0 = prep_l0_inputs(inputs)
    W = np.asarray(inputs["L1_attn_W"])
    proj = np.asarray(inputs["L1_attn_proj"])
    ccw = np.asarray(inputs["L1_attn_concat_w"])
    cbv = np.asarray(inputs["L1_attn_concat_b"])
    cent = np.asarray(inputs["L1_ffn_centroids"])
    ebias = np.asarray(inputs["L1_ffn_expert_biases"])
    wsup = np.asarray(inputs["L1_ffn_Ws_up"])
    wsdn = np.asarray(inputs["L1_ffn_Ws_down"])
    wup = np.asarray(inputs["L1_ffn_W_up"])
    wdn = np.asarray(inputs["L1_ffn_W_down"])
    shared = dict(
        W=_f(W),
        proj=_f(proj.reshape(NCH, 128).T),
        ccT=_f(ccw.T),
        cb=_f(cbv.reshape(NCH, 128).T),
        wsup=_f(wsup),
        wsdn=_f(wsdn),
        cent=_f(cent.T),
        bias_bc=_f(np.tile(ebias, (128, 1))),
        ident=_f(np.eye(128)),
        wup=_f(wup.reshape(NEXP * D, FEXP)),
        wdn=_f(wdn.reshape(NEXP * FEXP, D)),
    )
    for m in maps0:
        m.update(shared)
    return maps0


def kernel_unfused(**inputs):
    cores = list(range(NCORE))
    nc0 = build_l0()
    res0 = run_bass_kernel_spmd(nc0, prep_l0_inputs(inputs), core_ids=cores)
    x1pc = [np.asarray(r["x1"]) for r in res0.results]
    nc1 = build_l1a()
    res1 = run_bass_kernel_spmd(nc1, prep_l1a_inputs(inputs, x1pc),
                                core_ids=cores)
    nc2 = build_l1b()
    res2 = run_bass_kernel_spmd(nc2, prep_l1b_inputs(inputs, res1.results),
                                core_ids=cores)
    return assemble_fm(res2.results, "out")


def kernel(**inputs):
    cores = list(range(NCORE))
    nc = build_fused()
    res = run_bass_kernel_spmd(nc, prep_fused_inputs(inputs), core_ids=cores)
    return assemble_fm(res.results, "out")


# version 20
# speedup vs baseline: 1.2633x; 1.0020x over previous
import numpy as np
from contextlib import ExitStack

import concourse.bass as bass
import concourse.mybir as mybir
from concourse.bass_utils import run_bass_kernel_spmd

F32 = mybir.dt.float32
BF16 = mybir.dt.bfloat16
AF = mybir.ActivationFunctionType
ALU = mybir.AluOpType
AXX = mybir.AxisListType.X

NCORE = 8
D = 1024
NCH = 8
NT = 4096
TW = 512
NTILE = 8
NB = 8
PL = 512
NEXP = 64
FEXP = 512
TOPK = 8
NEG_BIG = -1e30


class Buf:
    __slots__ = ("name", "writer", "readers", "pending")
    def __init__(self, name=""):
        self.name = name
        self.writer = None
        self.readers = {}
        self.pending = False

ENGS = ("sync", "tensor", "vector", "scalar", "gpsimd")
ENG_SEM = {"tensor": "sP", "vector": "sV", "scalar": "sA", "gpsimd": "sG",
           "sync": "sS"}
QUEUES = {"q": ("sync", ["q0", "q1", "q2", "q3"]),
          "a": ("scalar", ["a0", "a1"]),
          "g": ("gpsimd", ["g0", "g1", "g2", "g3"])}


class Prog:
    def __init__(self, nc, ctx):
        self.nc = nc
        self.ctx = ctx
        self.streams = {e: [] for e in ENGS}
        self.counts = {}
        self.waited = {e: {} for e in ENGS}
        self.sems = {}
        self.qidx = {q: 0 for q in QUEUES}
        self.pending_bufs = {e: [] for e in ENGS}
        self.pending_reads = {e: [] for e in ENGS}
        names = list(ENG_SEM.values()) + sum((p for _, p in QUEUES.values()), []) + ["cc"]
        for n in names:
            self.sems[n] = ctx.enter_context(nc.semaphore(n))
            self.counts[n] = 0

    def _emit(self, eng, fn):
        self.streams[eng].append(fn)

    def _wait(self, eng, sname, val):
        if val <= 0:
            return
        if self.waited[eng].get(sname, 0) >= val:
            return
        self.waited[eng][sname] = val
        h = self.sems[sname]
        self._emit(eng, lambda E, h=h, v=val: E.wait_ge(h, v))

    def _dep_waits(self, eng, reads, writes):
        for b in reads:
            if b.pending:
                assert b in self.pending_bufs[eng], \
                    f"cross-engine read of pending buf {b.name}"
                continue
            if b.writer:
                self._wait(eng, *b.writer)
        for b in writes:
            if b.pending:
                assert b in self.pending_bufs[eng], \
                    f"cross-engine write of pending buf {b.name}"
                continue
            if b.writer:
                self._wait(eng, *b.writer)
            for s, v in b.readers.items():
                self._wait(eng, s, v)

    def _finish(self, token, reads, writes):
        s, v = token
        for b in reads:
            if b.readers.get(s, 0) < v:
                b.readers[s] = v
        for b in writes:
            b.writer = token
            b.readers = {}
            b.pending = False

    def op(self, eng, fn, reads=(), writes=(), rw=()):
        reads = list(reads) + list(rw)
        writes = list(writes) + list(rw)
        self._dep_waits(eng, reads, writes)
        s = ENG_SEM[eng]
        self._wait(eng, s, self.counts[s])
        self.counts[s] += 1
        tgt = self.counts[s]
        h = self.sems[s]
        self._emit(eng, lambda E, fn=fn, h=h: fn(E).then_inc(h, 1))
        token = (s, tgt)
        for b in self.pending_bufs[eng]:
            b.writer = token
            b.readers = {}
            b.pending = False
        self.pending_bufs[eng] = []
        for b in self.pending_reads[eng]:
            if b.readers.get(s, 0) < tgt:
                b.readers[s] = tgt
        self.pending_reads[eng] = []
        self._finish(token, reads, writes)
        return token

    def gop(self, eng, fn, reads=(), writes=(), rw=()):
        reads = list(reads) + list(rw)
        writes = list(writes) + list(rw)
        self._dep_waits(eng, reads, writes)
        self._emit(eng, fn)
        for b in writes:
            b.pending = True
            if b not in self.pending_bufs[eng]:
                self.pending_bufs[eng].append(b)
        for b in reads:
            if b not in self.pending_reads[eng]:
                self.pending_reads[eng].append(b)

    def dma(self, q, fn, reads=(), writes=(), rw=()):
        eng, pool = QUEUES[q]
        reads = list(reads) + list(rw)
        writes = list(writes) + list(rw)
        self._dep_waits(eng, reads, writes)
        s = pool[self.qidx[q] % len(pool)]
        self.qidx[q] += 1
        self._wait(eng, s, self.counts[s])
        self.counts[s] += 16
        tgt = self.counts[s]
        h = self.sems[s]
        self._emit(eng, lambda E, fn=fn, h=h: fn(E).then_inc(h, 16))
        token = (s, tgt)
        self._finish(token, reads, writes)
        return token

    def barrier_token(self, q):
        return [(s, self.counts[s]) for s in QUEUES[q][1] if self.counts[s]]

    def barrier(self):
        for e in ENGS:
            assert not self.pending_bufs[e] and not self.pending_reads[e]
        for e in ENGS:
            for s, v in sorted(self.counts.items()):
                if v:
                    self._wait(e, s, v)

    def cst(self, val):
        if not hasattr(self, "_consts"):
            self._consts = {}
        if val not in self._consts:
            i = len(self._consts)
            t = self.ctx.enter_context(
                self.nc.sbuf_tensor(f"cst{i}", [1, 1], F32))
            b = Buf(f"cst{val}")
            self.op("vector", lambda E, t=t, val=val: E.memset(t[0:1, 0:1], val),
                    writes=[b])
            self.nc.const_aps.register(F32, float(val), t[0:1, 0:1])
            self._consts[val] = b
        return self._consts[val]

    def finalize(self, out_tokens):
        nc = self.nc
        with nc.Block() as block:
            for eng in ENGS:
                stream = self.streams[eng]
                if eng == "sync":
                    extra = []
                    for s, v in out_tokens:
                        h = self.sems[s]
                        extra.append(lambda E, h=h, v=v: E.wait_ge(h, v))
                    stream = stream + extra
                if not stream:
                    continue
                sec = getattr(block, eng)
                def run(E, stream=stream):
                    for f in stream:
                        f(E)
                sec(run)


def ap(t, offset, dims):
    return bass.AP(t, offset, [list(d) for d in dims])



L0_KEYS = ["x", "halo", "bwT", "wrT", "wkT", "wvT", "cwT", "logA", "dpar",
           "mur", "muk", "onec", "hpick", "x1", "z0", "cardr", "carga"]

def _decl_l0_drams(nc, fused=False):
    T = {}
    T["x"] = nc.dram_tensor("x", [128, NTILE * NCH * TW], F32, kind="ExternalInput")
    T["halo"] = nc.dram_tensor("halo", [128, NCH * NB], F32, kind="ExternalInput")
    T["bwT"] = nc.dram_tensor("bwT", [D, 128], F32, kind="ExternalInput")
    T["wrT"] = nc.dram_tensor("wrT", [D, D], F32, kind="ExternalInput")
    T["wkT"] = nc.dram_tensor("wkT", [D, D], F32, kind="ExternalInput")
    T["wvT"] = nc.dram_tensor("wvT", [D, D], F32, kind="ExternalInput")
    T["cwT"] = nc.dram_tensor("cwT", [128, D], F32, kind="ExternalInput")
    T["logA"] = nc.dram_tensor("logA", [128, 1], F32, kind="ExternalInput")
    T["dpar"] = nc.dram_tensor("dpar", [128, NCH], F32, kind="ExternalInput")
    T["mur"] = nc.dram_tensor("mur", [128, NCH], F32, kind="ExternalInput")
    T["muk"] = nc.dram_tensor("muk", [128, NCH], F32, kind="ExternalInput")
    T["onec"] = nc.dram_tensor("onec", [128, 1], F32, kind="ExternalInput")
    T["hpick"] = nc.dram_tensor("hpick", [128, NCORE * NB], F32, kind="ExternalInput")
    T["x1"] = nc.dram_tensor("x1", [128, NTILE * NCH * TW], F32,
                             kind="Internal" if fused else "ExternalOutput")
    T["z0"] = nc.dram_tensor("z0", [128, NTILE * NCH * TW], F32, kind="Internal")
    T["cardr"] = nc.dram_tensor("cardr", [128, NB], F32, kind="Internal")
    T["carga"] = nc.dram_tensor("carga", [NCORE * 128, NB], F32, kind="Internal")
    return T


def emit_l0(nc, p, T, ctx, pre="l0_"):
    x_h = T["x"]; halo_h = T["halo"]; bwT_h = T["bwT"]; wrT_h = T["wrT"]
    wkT_h = T["wkT"]; wvT_h = T["wvT"]; cwT_h = T["cwT"]; logA_h = T["logA"]
    dpar_h = T["dpar"]; mur_h = T["mur"]; muk_h = T["muk"]
    onec_h = T["onec"]; hpick_h = T["hpick"]; x1_h = T["x1"]; z0_h = T["z0"]
    cardr_h = T["cardr"]; carga_h = T["carga"]

    sb = lambda name, shape, dt: ctx.enter_context(
        nc.sbuf_tensor(pre + name, shape, dt))
    psb = lambda name, shape, dt=F32: ctx.enter_context(
        nc.psum_tensor(pre + name, shape, dt))

    bwT = sb("bwT_b", [128, NCH * 128], BF16)
    wrT = sb("wrT_b", [128, NCH * D], BF16)
    wkT = sb("wkT_b", [128, NCH * D], BF16)
    wvT = sb("wvT_b", [128, NCH * D], BF16)
    cwT = sb("cwT_b", [128, D], F32)
    logA = sb("logA_s", [128, 1], F32)
    dpar = sb("dpar_s", [128, NCH], F32)
    mur = sb("mur_s", [128, NCH], F32)
    muk = sb("muk_s", [128, NCH], F32)
    onec = sb("onec_s", [128, 1], F32)
    hpick = sb("hpick_s", [128, NCORE * NB], F32)
    halo = sb("halo_s", [128, NCH * NB], F32)
    haloxn = sb("haloxn_s", [128, NCH * NB], F32)

    xsl = [sb(f"xsl{i}", [128, NCH * TW], F32) for i in range(3)]
    sq = sb("sq", [128, 2 * TW], F32)
    xn5 = [sb(f"xn5_{i}", [128, NCH * (TW + 1)], BF16) for i in range(2)]
    xrb = [sb(f"xrb{i}", [128, NCH * TW], BF16) for i in range(2)]
    xkb = [sb(f"xkb{i}", [128, NCH * TW], BF16) for i in range(2)]
    kab = sb("kab", [128, NCH * TW], BF16)
    rsl = sb("rsl", [128, 2 * TW], F32)
    dif = sb("dif", [128, TW], BF16)
    tmpv = sb("tmpv", [128, TW], F32)
    mean = sb("mean", [1, TW], F32)
    msq = sb("msq", [1, TW], F32)
    var = sb("var", [1, TW], F32)
    varl = sb("varl", [1, TW], F32)
    rstd = sb("rstd", [1, TW], F32)
    mrs = sb("mrs", [1, TW], F32)
    rstd_bc = sb("rstd_bc", [128, TW], F32)
    mrs_bc = sb("mrs_bc", [128, TW], F32)
    hrow = sb("hrow", [1, NB], F32)
    hrow2 = sb("hrow2", [1, NB], F32)
    hvar = sb("hvar", [1, NB], F32)
    hrstd = sb("hrstd", [1, NB], F32)
    hmrs = sb("hmrs", [1, NB], F32)
    hrstd_bc = sb("hrstd_bc", [128, NB], F32)
    hmrs_bc = sb("hmrs_bc", [128, NB], F32)
    xp = sb("xp", [128, NT], F32)
    acol = sb("acol", [128, 1], F32)
    a512 = sb("a512", [128, 1], F32)
    apow = sb("apow", [128, TW], F32)
    carry = sb("carry", [128, NB], F32)
    carga = sb("carga_s", [128, NCORE * NB], F32)
    hscan = sb("hscan", [128, NCORE * NB], F32)
    hin = sb("hin", [128, NB], F32)

    ps_su = psb("ps_su", [1, TW])
    ps_sq = psb("ps_sq", [1, TW])
    ps_hsu = psb("ps_hsu", [1, NCH * NB])
    ps_hsq = psb("ps_hsq", [1, NCH * NB])
    ps_xp = psb("ps_xp", [128, TW])
    ps_m0 = psb("ps_m0", [128, TW])
    ps_m1 = psb("ps_m1", [128, TW])


    B = lambda n: Buf(n)
    b_w = {n: B(n) for n in ["bwT", "wrT", "wkT", "wvT", "cwT", "logA", "dpar",
                             "mur", "muk", "onec", "hpick", "halo"]}
    b_haloxn = B("haloxn")
    b_hrows = B("hrows")
    b_xsl = [[B(f"xsl{i}s{k}") for k in range(4)] for i in range(3)]
    b_sq = [B(f"sq{i}") for i in range(2)]
    b_xn5 = [[B(f"xn5_{i}h{h}") for h in range(2)] for i in range(2)]
    b_xrb = [B(f"xrb{i}") for i in range(2)]
    b_xkb = [B(f"xkb{i}") for i in range(2)]
    b_kab = [B(f"kab{i}") for i in range(NCH)]
    b_rsl = [B(f"rsl{i}") for i in range(2)]
    b_dif = B("dif")
    b_tmpv = B("tmpv")
    b_stat = B("stat")
    b_bc = B("bc")
    b_xp = B("xp")
    b_acol = B("acol")
    b_apow = B("apow")
    b_carry = B("carry")
    b_carga = B("carga")
    b_hscan = B("hscan")
    b_hin = B("hin")
    b_ps_su = B("ps_su")
    b_ps_sq = B("ps_sq")
    b_ps_hsu = B("ps_hsu")
    b_ps_hsq = B("ps_hsq")
    b_ps_xp = B("ps_xp")
    b_ps_m = [B("ps_m0"), B("ps_m1")]
    b_z0 = [[B(f"z0_{j}s{k}") for k in range(4)] for j in range(NTILE)]
    b_x1 = [[B(f"x1_{j}s{k}") for k in range(4)] for j in range(NTILE)]
    b_cardr = B("cardr")
    b_carga_d = B("carga_d")

    p.dma("g", lambda E: E.dma_start(
        out=ap(bwT, 0, [[NCH * 128, 128], [128, NCH], [1, 128]]),
        in_=ap(bwT_h, 0, [[128, 128], [128 * 128, NCH], [1, 128]])),
        writes=[b_w["bwT"]])
    for name, hb, sbuf, key in [("wrT", wrT_h, wrT, "wrT"),
                                ("wkT", wkT_h, wkT, "wkT"),
                                ("wvT", wvT_h, wvT, "wvT")]:
        p.dma("g", lambda E, hb=hb, sbuf=sbuf: E.dma_start(
            out=ap(sbuf, 0, [[NCH * D, 128], [D, NCH], [1, D]]),
            in_=ap(hb, 0, [[D, 128], [128 * D, NCH], [1, D]])),
            writes=[b_w[key]])
    p.dma("q", lambda E: E.dma_start(out=cwT[:, :], in_=cwT_h[:, :]),
          writes=[b_w["cwT"]])
    for hb, sbuf, key in [(logA_h, logA, "logA"), (dpar_h, dpar, "dpar"),
                          (mur_h, mur, "mur"), (muk_h, muk, "muk"),
                          (onec_h, onec, "onec"),
                          (hpick_h, hpick, "hpick"), (halo_h, halo, "halo")]:
        p.dma("q", lambda E, hb=hb, sbuf=sbuf: E.dma_start(
            out=sbuf[:, :], in_=hb[:, :]), writes=[b_w[key]])

    p.op("tensor", lambda E: E.matmul(ps_hsu[0:1, :], onec[:, 0:1],
                                      halo[:, :], start=True, stop=True),
         reads=[b_w["onec"], b_w["halo"]], writes=[b_ps_hsu])
    p.op("scalar", lambda E: E.activation(sq[:, 0:NCH * NB], halo[:, :],
                                          AF.Square),
         reads=[b_w["halo"]], writes=[b_sq[0]])
    p.op("tensor", lambda E: E.matmul(ps_hsq[0:1, :], onec[:, 0:1],
                                      sq[:, 0:NCH * NB], start=True, stop=True),
         reads=[b_w["onec"], b_sq[0]], writes=[b_ps_hsq])
    p.op("vector", lambda E: E.reduce_sum(
        hrow[0:1, :], ap(ps_hsu, 0, [[NB * NCH, 1], [1, NB], [NB, NCH]]),
        axis=AXX), reads=[b_ps_hsu], writes=[b_hrows])
    p.op("vector", lambda E: E.reduce_sum(
        hrow2[0:1, :], ap(ps_hsq, 0, [[NB * NCH, 1], [1, NB], [NB, NCH]]),
        axis=AXX), reads=[b_ps_hsq], writes=[b_hrows])
    p.op("scalar", lambda E: E.activation(hrow[0:1, :], hrow[0:1, :], AF.Copy,
                                          scale=1.0 / D),
         reads=[p.cst(1.0 / D)], rw=[b_hrows])
    p.op("scalar", lambda E: E.activation(hrow2[0:1, :], hrow2[0:1, :], AF.Copy,
                                          scale=1.0 / D),
         reads=[p.cst(1.0 / D)], rw=[b_hrows])
    p.op("vector", lambda E: E.tensor_mul(hvar[0:1, :], hrow[0:1, :],
                                          hrow[0:1, :]),
         reads=[b_hrows], writes=[b_hrows])
    p.op("vector", lambda E: E.tensor_sub(hvar[0:1, :], hrow2[0:1, :],
                                          hvar[0:1, :]), rw=[b_hrows])
    p.op("scalar", lambda E: E.activation(hvar[0:1, :], hvar[0:1, :], AF.Ln,
                                          bias=1e-5),
         reads=[p.cst(1e-5)], rw=[b_hrows])
    p.op("scalar", lambda E: E.activation(hrstd[0:1, :], hvar[0:1, :], AF.Exp,
                                          scale=-0.5),
         reads=[p.cst(-0.5)], rw=[b_hrows])
    p.op("vector", lambda E: E.tensor_mul(hmrs[0:1, :], hrow[0:1, :],
                                          hrstd[0:1, :]), rw=[b_hrows])
    p.dma("q", lambda E: E.dma_start(out=hrstd_bc[:, :],
                                     in_=ap(hrstd, 0, [[NB, 1], [0, 128], [1, NB]])),
          reads=[b_hrows], writes=[b_hrows])
    p.dma("q", lambda E: E.dma_start(out=hmrs_bc[:, :],
                                     in_=ap(hmrs, 0, [[NB, 1], [0, 128], [1, NB]])),
          reads=[b_hrows], writes=[b_hrows])
    p.op("vector", lambda E: E.tensor_mul(
        haloxn[:, :], halo[:, :],
        ap(hrstd_bc, 0, [[NB, 128], [0, NCH], [1, NB]])),
        reads=[b_w["halo"], b_hrows], writes=[b_haloxn])
    p.op("vector", lambda E: E.tensor_sub(
        haloxn[:, :], haloxn[:, :],
        ap(hmrs_bc, 0, [[NB, 128], [0, NCH], [1, NB]])),
        reads=[b_hrows], rw=[b_haloxn])

    def stage_ln_stats(j):
        par = j % 3
        xs, bxs = xsl[par], b_xsl[par]
        js = j * TW
        for k in range(4):
            p.dma("q", lambda E, js=js, xs=xs, k=k: E.dma_start(
                out=ap(xs, 2 * k * TW, [[NCH * TW, 128], [1, 2 * TW]]),
                in_=ap(x_h, js * NCH + 2 * k * TW,
                       [[NTILE * NCH * TW, 128], [1, 2 * TW]])),
                writes=[bxs[k]])
        for ci in range(NCH):
            fn = lambda E, ci=ci, xs=xs: E.matmul(
                ps_su[0:1, :], onec[:, 0:1], xs[:, ci * TW:(ci + 1) * TW],
                start=(ci == 0), stop=(ci == NCH - 1))
            if ci < NCH - 1:
                p.gop("tensor", fn, reads=[b_w["onec"], bxs[ci // 2]], writes=[b_ps_su])
            else:
                p.op("tensor", fn, reads=[b_w["onec"], bxs[ci // 2]], writes=[b_ps_su])
        for ci in range(NCH):
            sl = ci % 2
            p.op("scalar", lambda E, ci=ci, sl=sl, xs=xs: E.activation(
                sq[:, sl * TW:(sl + 1) * TW], xs[:, ci * TW:(ci + 1) * TW],
                AF.Square), reads=[bxs[ci // 2]], writes=[b_sq[sl]])
            p.op("tensor", lambda E, ci=ci, sl=sl: E.matmul(
                ps_sq[0:1, :], onec[:, 0:1], sq[:, sl * TW:(sl + 1) * TW],
                start=(ci == 0), stop=(ci == NCH - 1)),
                reads=[b_w["onec"], b_sq[sl]], writes=[b_ps_sq])
        p.op("scalar", lambda E: E.activation(mean[0:1, :], ps_su[0:1, :],
                                              AF.Copy, scale=1.0 / D),
             reads=[b_ps_su, p.cst(1.0 / D)], writes=[b_stat])
        p.op("scalar", lambda E: E.activation(msq[0:1, :], ps_sq[0:1, :],
                                              AF.Copy, scale=1.0 / D),
             reads=[b_ps_sq, p.cst(1.0 / D)], writes=[b_stat])
        p.op("vector", lambda E: E.tensor_mul(var[0:1, :], mean[0:1, :],
                                              mean[0:1, :]), rw=[b_stat])
        p.op("vector", lambda E: E.tensor_sub(var[0:1, :], msq[0:1, :],
                                              var[0:1, :]), rw=[b_stat])
        p.op("scalar", lambda E: E.activation(varl[0:1, :], var[0:1, :],
                                              AF.Ln, bias=1e-5),
             reads=[p.cst(1e-5)], rw=[b_stat])
        p.op("scalar", lambda E: E.activation(rstd[0:1, :], varl[0:1, :],
                                              AF.Exp, scale=-0.5),
             reads=[p.cst(-0.5)], rw=[b_stat])
        p.op("vector", lambda E: E.tensor_mul(mrs[0:1, :], mean[0:1, :],
                                              rstd[0:1, :]), rw=[b_stat])
        p.dma("q", lambda E: E.dma_start(
            out=rstd_bc[:, :], in_=ap(rstd, 0, [[TW, 1], [0, 128], [1, TW]])),
            reads=[b_stat], writes=[b_bc])
        p.dma("q", lambda E: E.dma_start(
            out=mrs_bc[:, :], in_=ap(mrs, 0, [[TW, 1], [0, 128], [1, TW]])),
            reads=[b_stat], writes=[b_bc])
    def stage_ln_apply(j):
        par = j % 2
        xs, bxs = xsl[j % 3], b_xsl[j % 3]
        x5 = xn5[par]
        for ci in range(NCH):
            c5 = ci * (TW + 1)
            half = ci // 4
            eng = "gpsimd" if half == 0 else "vector"
            bx5 = b_xn5[par][half]
            p.op(eng, lambda E, ci=ci, c5=c5, xs=xs, x5=x5: E.tensor_mul(
                x5[:, c5 + 1:c5 + 1 + TW], xs[:, ci * TW:(ci + 1) * TW],
                rstd_bc[:, :]), reads=[bxs[ci // 2], b_bc], writes=[bx5])
            p.op(eng, lambda E, ci=ci, c5=c5, x5=x5: E.tensor_sub(
                x5[:, c5 + 1:c5 + 1 + TW], x5[:, c5 + 1:c5 + 1 + TW],
                mrs_bc[:, :]), reads=[b_bc], rw=[bx5])
            p.op(eng, lambda E, ci=ci, c5=c5, j=j, x5=x5: E.tensor_copy(
                x5[:, c5:c5 + 1], haloxn[:, ci * NB + j:ci * NB + j + 1]),
                reads=[b_haloxn], writes=[bx5])
        for ci in range(NCH):
            c5 = ci * (TW + 1)
            bx5 = b_xn5[par][ci // 4]
            p.op("vector", lambda E, ci=ci, c5=c5, xs=xs, x5=x5: E.scalar_tensor_tensor(
                xs[:, ci * TW:(ci + 1) * TW], x5[:, c5 + 1:c5 + 1 + TW],
                dpar[:, ci:ci + 1], xs[:, ci * TW:(ci + 1) * TW],
                ALU.mult, ALU.add),
                reads=[bx5, b_w["dpar"]], rw=[bxs[ci // 2]])
        for ci in range(NCH):
            c5 = ci * (TW + 1)
            bx5 = b_xn5[par][ci // 4]
            p.op("vector", lambda E, ci=ci, c5=c5, x5=x5: E.tensor_sub(
                dif[:, :], x5[:, c5 + 1:c5 + 1 + TW], x5[:, c5:c5 + TW]),
                reads=[bx5], writes=[b_dif])
            p.op("vector", lambda E, ci=ci, c5=c5, par=par, x5=x5: E.scalar_tensor_tensor(
                xrb[par][:, ci * TW:(ci + 1) * TW], dif[:, :], mur[:, ci:ci + 1],
                x5[:, c5:c5 + TW], ALU.mult, ALU.add),
                reads=[b_dif, b_w["mur"], bx5], writes=[b_xrb[par]])
            p.op("vector", lambda E, ci=ci, c5=c5, par=par, x5=x5: E.scalar_tensor_tensor(
                xkb[par][:, ci * TW:(ci + 1) * TW], dif[:, :], muk[:, ci:ci + 1],
                x5[:, c5:c5 + TW], ALU.mult, ALU.add),
                reads=[b_dif, b_w["muk"], bx5], writes=[b_xkb[par]])

    def stage_mm(j):
        par = j % 2
        xs, bxs = xsl[j % 3], b_xsl[j % 3]
        js = j * TW
        x5 = xn5[par]
        for ci in range(NCH):
            c5 = ci * (TW + 1)
            bx5 = b_xn5[par][ci // 4]
            fn = lambda E, ci=ci, c5=c5, x5=x5: E.matmul(
                ps_xp[:, :], bwT[:, ci * 128:(ci + 1) * 128],
                x5[:, c5 + 1:c5 + 1 + TW],
                start=(ci == 0), stop=(ci == NCH - 1))
            if ci < NCH - 1:
                p.gop("tensor", fn, reads=[b_w["bwT"], bx5], writes=[b_ps_xp])
            else:
                p.op("tensor", fn, reads=[b_w["bwT"], bx5], writes=[b_ps_xp])
        p.op("vector", lambda E, js=js: E.tensor_copy(
            xp[:, js:js + TW], ps_xp[:, :]), reads=[b_ps_xp], writes=[b_xp])
        for dj in range(NCH):
            pm, bpm = (ps_m0, b_ps_m[0]) if dj % 2 == 0 else (ps_m1, b_ps_m[1])
            for ci in range(NCH):
                fn = lambda E, ci=ci, dj=dj, pm=pm, par=par: E.matmul(
                    pm[:, :], wkT[:, ci * D + dj * 128:ci * D + (dj + 1) * 128],
                    xkb[par][:, ci * TW:(ci + 1) * TW],
                    start=(ci == 0), stop=(ci == NCH - 1))
                if ci < NCH - 1:
                    p.gop("tensor", fn, reads=[b_w["wkT"], b_xkb[par]], writes=[bpm])
                else:
                    p.op("tensor", fn, reads=[b_w["wkT"], b_xkb[par]], writes=[bpm])
            p.op("scalar", lambda E, pm=pm: E.activation(
                tmpv[:, :], pm[:, :], AF.Relu), reads=[bpm], writes=[b_tmpv])
            p.op("scalar", lambda E, dj=dj: E.activation(
                kab[:, dj * TW:(dj + 1) * TW], tmpv[:, :], AF.Square),
                reads=[b_tmpv], writes=[b_kab[dj]])
        for dj in range(NCH):
            sl = dj % 2
            pmr, bpmr = (ps_m0, b_ps_m[0]) if sl == 0 else (ps_m1, b_ps_m[1])
            pmv, bpmv = (ps_m1, b_ps_m[1]) if sl == 0 else (ps_m0, b_ps_m[0])
            for ci in range(NCH):
                fn = lambda E, ci=ci, dj=dj, pm=pmr, par=par: E.matmul(
                    pm[:, :], wrT[:, ci * D + dj * 128:ci * D + (dj + 1) * 128],
                    xrb[par][:, ci * TW:(ci + 1) * TW],
                    start=(ci == 0), stop=(ci == NCH - 1))
                if ci < NCH - 1:
                    p.gop("tensor", fn, reads=[b_w["wrT"], b_xrb[par]], writes=[bpmr])
                else:
                    p.op("tensor", fn, reads=[b_w["wrT"], b_xrb[par]], writes=[bpmr])
            p.op("scalar", lambda E, sl=sl, pm=pmr: E.activation(
                rsl[:, sl * TW:(sl + 1) * TW], pm[:, :], AF.Sigmoid),
                reads=[bpmr], writes=[b_rsl[sl]])
            for ci in range(NCH):
                fn = lambda E, ci=ci, dj=dj, pm=pmv: E.matmul(
                    pm[:, :], wvT[:, ci * D + dj * 128:ci * D + (dj + 1) * 128],
                    kab[:, ci * TW:(ci + 1) * TW],
                    start=(ci == 0), stop=(ci == NCH - 1))
                if ci < NCH - 1:
                    p.gop("tensor", fn, reads=[b_w["wvT"], b_kab[ci]], writes=[bpmv])
                else:
                    p.op("tensor", fn, reads=[b_w["wvT"], b_kab[ci]], writes=[bpmv])
            p.op("vector", lambda E, pm=pmv, sl=sl: E.tensor_mul(
                tmpv[:, :], pm[:, :], rsl[:, sl * TW:(sl + 1) * TW]),
                reads=[bpmv, b_rsl[sl]], writes=[b_tmpv])
            p.op("vector", lambda E, dj=dj, xs=xs: E.tensor_add(
                xs[:, dj * TW:(dj + 1) * TW], xs[:, dj * TW:(dj + 1) * TW],
                tmpv[:, :]), reads=[b_tmpv], rw=[bxs[dj // 2]])
        for k, qq in enumerate(("a", "a", "g", "g")):
            p.dma(qq, lambda E, js=js, xs=xs, j=j, k=k: E.dma_start(
                out=ap(z0_h, js * NCH + 2 * k * TW,
                       [[NTILE * NCH * TW, 128], [1, 2 * TW]]),
                in_=ap(xs, 2 * k * TW, [[NCH * TW, 128], [1, 2 * TW]])),
                reads=[bxs[k]], writes=[b_z0[j][k]])

    stage_ln_stats(0)
    stage_ln_apply(0)
    for j in range(NTILE):
        if j + 1 < NTILE:
            stage_ln_stats(j + 1)
            stage_ln_apply(j + 1)
        stage_mm(j)

    p.op("scalar", lambda E: E.activation(acol[:, :], logA[:, :], AF.Exp),
         reads=[b_w["logA"]], writes=[b_acol])
    p.op("scalar", lambda E: E.activation(a512[:, :], logA[:, :], AF.Exp,
                                          scale=512.0),
         reads=[b_w["logA"], p.cst(512.0)], writes=[b_acol])
    p.op("vector", lambda E: E.tensor_tensor_scan(
        apow[:, :], ap(acol, 0, [[1, 128], [0, TW]]),
        ap(acol, 0, [[1, 128], [0, TW]]), 1.0, ALU.mult, ALU.bypass),
        reads=[b_acol], writes=[b_apow])
    for b in range(NB):
        p.op("vector", lambda E, b=b: E.tensor_tensor_scan(
            xp[:, b * PL:(b + 1) * PL],
            ap(acol, 0, [[1, 128], [0, PL]]),
            xp[:, b * PL:(b + 1) * PL], 0.0, ALU.mult, ALU.add),
            reads=[b_acol], rw=[b_xp])
    for b in range(NB):
        p.op("vector", lambda E, b=b: E.tensor_copy(
            carry[:, b:b + 1], xp[:, b * PL + PL - 1:b * PL + PL]),
            reads=[b_xp], writes=[b_carry])
    p.dma("g", lambda E: E.dma_start(out=cardr_h[:, :], in_=carry[:, :]),
          reads=[b_carry], writes=[b_cardr])
    p.op("gpsimd", lambda E: E.collective_compute(
        "AllGather", ALU.bypass, replica_groups=[list(range(NCORE))],
        ins=[cardr_h.ap()], outs=[carga_h.ap()]),
        reads=[b_cardr], writes=[b_carga_d])
    p.dma("g", lambda E: E.dma_start(
        out=carga[:, :],
        in_=ap(carga_h, 0, [[NB, 128], [128 * NB, NCORE], [1, NB]])),
        reads=[b_carga_d], writes=[b_carga])
    for b in range(NB):
        p.op("vector", lambda E, b=b: E.tensor_tensor_scan(
            ap(hscan, b, [[NCORE * NB, 128], [NB, NCORE]]),
            ap(a512, 0, [[1, 128], [0, NCORE]]),
            ap(carga, b, [[NCORE * NB, 128], [NB, NCORE]]),
            0.0, ALU.mult, ALU.add),
            reads=[b_acol, b_carga], writes=[b_hscan])
    p.op("vector", lambda E: E.tensor_mul(hscan[:, :], hscan[:, :],
                                          hpick[:, :]),
         reads=[b_w["hpick"]], rw=[b_hscan])
    for b in range(NB):
        p.op("vector", lambda E, b=b: E.reduce_sum(
            hin[:, b:b + 1], ap(hscan, b, [[NCORE * NB, 128], [NB, NCORE]]),
            axis=AXX), reads=[b_hscan], writes=[b_hin])
    for b in range(NB):
        p.op("vector", lambda E, b=b: E.scalar_tensor_tensor(
            xp[:, b * PL:(b + 1) * PL], apow[:, :], hin[:, b:b + 1],
            xp[:, b * PL:(b + 1) * PL], ALU.mult, ALU.add),
            reads=[b_apow, b_hin], rw=[b_xp])


    for j in range(NTILE):
        js = j * TW
        par = j % 3
        xs, bxs = xsl[par], b_xsl[par]
        for k in range(4):
            p.dma("q", lambda E, js=js, xs=xs, k=k: E.dma_start(
                out=ap(xs, 2 * k * TW, [[NCH * TW, 128], [1, 2 * TW]]),
                in_=ap(z0_h, js * NCH + 2 * k * TW,
                       [[NTILE * NCH * TW, 128], [1, 2 * TW]])),
                reads=[b_z0[j][k]], writes=[bxs[k]])
        for dj in range(NCH):
            pm, bpm = (ps_m0, b_ps_m[0]) if dj % 2 == 0 else (ps_m1, b_ps_m[1])
            p.op("tensor", lambda E, dj=dj, js=js, pm=pm: E.matmul(
                pm[:, :], cwT[:, dj * 128:(dj + 1) * 128],
                xp[:, js:js + TW], start=True, stop=True),
                reads=[b_w["cwT"], b_xp], writes=[bpm])
            p.op("vector", lambda E, dj=dj, pm=pm, xs=xs: E.scalar_tensor_tensor(
                xs[:, dj * TW:(dj + 1) * TW], pm[:, :], 1.0,
                xs[:, dj * TW:(dj + 1) * TW], ALU.mult, ALU.add),
                reads=[bpm], rw=[bxs[dj // 2]])
        for k, qq in enumerate(("a", "a", "g", "g")):
            p.dma(qq, lambda E, js=js, xs=xs, k=k: E.dma_start(
                out=ap(x1_h, js * NCH + 2 * k * TW,
                       [[NTILE * NCH * TW, 128], [1, 2 * TW]]),
                in_=ap(xs, 2 * k * TW, [[NCH * TW, 128], [1, 2 * TW]])),
                reads=[bxs[k]], writes=[b_x1[j][k]])

    return [b_x1[j][k].writer for j in range(NTILE) for k in range(4)]


def build_l0():
    nc = bass.Bass("TRN2", target_bir_lowering=False, debug=False)
    ctx = ExitStack()
    T = _decl_l0_drams(nc, fused=False)
    p = Prog(nc, ctx)
    toks = emit_l0(nc, p, T, ctx)
    p.finalize(toks)
    ctx.close()
    return nc



L1A_KEYS = ["x1", "W", "proj", "ccT", "cb", "wsup", "wsdn", "cent",
            "bias_bc", "onec", "ident", "z", "xn1b", "cwm"]

def _decl_l1a_drams(nc, fused=False, T=None):
    if T is None:
        T = {}
    T["W"] = nc.dram_tensor("W", [D, D], F32, kind="ExternalInput")
    T["proj"] = nc.dram_tensor("proj", [128, NCH], F32, kind="ExternalInput")
    T["ccT"] = nc.dram_tensor("ccT", [2 * D, D], F32, kind="ExternalInput")
    T["cb"] = nc.dram_tensor("cb", [128, NCH], F32, kind="ExternalInput")
    T["wsup"] = nc.dram_tensor("wsup", [D, FEXP], F32, kind="ExternalInput")
    T["wsdn"] = nc.dram_tensor("wsdn", [FEXP, D], F32, kind="ExternalInput")
    T["cent"] = nc.dram_tensor("cent", [D, NEXP], F32, kind="ExternalInput")
    T["bias_bc"] = nc.dram_tensor("bias_bc", [128, NEXP], F32, kind="ExternalInput")
    T["ident"] = nc.dram_tensor("ident", [128, 128], F32, kind="ExternalInput")
    ko = "Internal" if fused else "ExternalOutput"
    T["z"] = nc.dram_tensor("z", [D, NT], F32, kind=ko)
    T["xn1b"] = nc.dram_tensor("xn1b", [D, NT], BF16, kind=ko)
    T["cwm"] = nc.dram_tensor("cwm", [NT, NEXP], F32, kind=ko)
    return T


def emit_l1a(nc, p, T, ctx, pre="l1a_"):
    x1_h = T["x1"]; W_h = T["W"]; proj_h = T["proj"]; ccT_h = T["ccT"]
    cb_h = T["cb"]; wsup_h = T["wsup"]; wsdn_h = T["wsdn"]; cent_h = T["cent"]
    bias_bc_h = T["bias_bc"]; onec_h = T["onec"]; ident_h = T["ident"]
    z_h = T["z"]; xn1b_h = T["xn1b"]; cwm_h = T["cwm"]

    sb = lambda name, shape, dt: ctx.enter_context(
        nc.sbuf_tensor(pre + name, shape, dt))
    psb = lambda name, shape, dt=F32: ctx.enter_context(
        nc.psum_tensor(pre + name, shape, dt))

    Wb = sb("Wb", [128, NCH * D], BF16)
    projb = sb("projb", [128, NCH], BF16)
    ccT = sb("ccTb", [128, 16 * D], BF16)
    cb = sb("cb_s", [128, NCH], F32)
    wsup = sb("wsup_b", [128, NCH * FEXP], BF16)
    wsdn = sb("wsdn_b", [128, 4 * D], BF16)
    cent = sb("cent_b", [128, NCH * NEXP], F32)
    bias_bc = sb("bias_bc_s", [128, NEXP], F32)
    onec = sb("onec_s", [128, 1], F32)

    xsl = [sb(f"xsl{i}", [128, NCH * TW], F32) for i in range(2)]
    xn1f = sb("xn1f", [128, NCH * TW], F32)
    xnb = sb("xnb", [128, NCH * TW], BF16)
    hsh = sb("hsh", [128, 4 * TW], BF16)
    sq = sb("sq", [128, 2 * TW], BF16)
    onecb = sb("onecb", [128, 1], BF16)
    mean = sb("mean", [1, TW], F32)
    msq = sb("msq", [1, TW], F32)
    var = sb("var", [1, TW], F32)
    varl = var
    rstd = msq
    mrs = mean
    rstd_bc = sb("rstd_bc", [128, TW], F32)
    mrs_bc = sb("mrs_bc", [128, TW], F32)
    aff = sb("aff", [128, NEXP], F32)
    top8 = sb("top8", [128, 8], F32)
    selm = sb("selm", [128, NEXP], F32)
    sgm = sb("sgm", [128, NEXP], F32)
    wsum = sb("wsum", [128, 1], F32)
    cwt = sb("cwt", [128, NEXP], F32)
    xpc = [sb(f"xpc{i}", [128, NCH * NB * 128], BF16) for i in range(2)]
    hpb = [sb(f"hpb{i}", [128, NCH * 128], BF16) for i in range(2)]
    scs = [sb(f"scs{i}", [128, NB], F32) for i in range(2)]
    tmpc = sb("tmpc", [128, NCH * 128], F32)
    emax = sb("emax", [128, 1], F32)
    gex = sb("gex", [128, 7], F32)
    cums = sb("cums", [128, 6], F32)
    rnv = sb("rnv", [128, 6], F32)
    trin = sb("trin", [128, 16], F32)
    trT = sb("trT", [16, 128], F32)
    gbc = sb("gbc", [128, 7 * 128], BF16)
    rnbc = sb("rnbc", [128, 6 * 128], F32)
    ctxa = sb("ctxa", [128, NCH * 128], F32)
    ctxb = sb("ctxb", [128, NCH * 6 * 128], BF16)
    attt = sb("attt", [128, NCH * 128], F32)
    ident = sb("ident_s", [128, 128], F32)
    tmp2 = sb("tmp2", [128, 128], F32)

    ps_su = psb("ps_su", [1, TW])
    ps_sq = psb("ps_sq", [1, TW])
    ps_m0 = psb("ps_m0", [128, TW])
    ps_m1 = psb("ps_m1", [128, TW])
    ps_aff = psb("ps_aff", [128, NEXP])
    ps_sc = [psb(f"ps_sc{i}", [128, NB]) for i in range(2)]
    ps_tr = psb("ps_tr", [16, 128])

    B = lambda n: Buf(n)
    b_w = {n: B(n) for n in ["W", "proj", "ccT", "cb", "wsup", "wsdn",
                             "cent", "bias_bc", "onec", "onecb", "ident"]}
    b_xsl = [[B(f"xsl{i}s{k}") for k in range(4)] for i in range(2)]
    b_xn1f = [B("xn1fa"), B("xn1fb")]
    b_xnb = [B("xnbh0"), B("xnbh1")]
    b_hsh = B("hsh")
    b_sq = [B(f"sq{i}") for i in range(2)]
    b_stat = B("stat")
    b_bc = B("bc")
    b_aff = B("aff")
    b_xpc = [B(f"xpc{i}") for i in range(2)]
    b_hpb = [B(f"hpb{i}") for i in range(2)]
    b_scs = [B(f"scs{i}") for i in range(2)]
    b_tmpc = B("tmpc")
    b_soft = B("soft")
    b_trT = B("trT")
    b_gbc = B("gbc")
    b_rnbc = B("rnbc")
    b_ctxa = B("ctxa")
    b_ctxb = B("ctxb")
    b_attt = B("attt")
    b_ps_su = B("ps_su")
    b_ps_sq = B("ps_sq")
    b_ps_m = [B("ps_m0"), B("ps_m1")]
    b_ps_aff = B("ps_aff")
    b_ps_sc = [B(f"ps_sc{i}") for i in range(2)]
    b_ps_tr = B("ps_tr")
    b_z = [[B(f"z{j}s{k}") for k in range(4)] for j in range(NTILE)]
    b_xn1b = [[B(f"xn1b{j}h{k}") for k in range(2)] for j in range(NTILE)]
    b_cwm = [B(f"cwm{j}") for j in range(NTILE)]

    p.dma("g", lambda E: E.dma_start(
        out=ap(Wb, 0, [[NCH * D, 128], [D, NCH], [1, D]]),
        in_=ap(W_h, 0, [[D, 128], [128 * D, NCH], [1, D]])),
        writes=[b_w["W"]])
    p.dma("g", lambda E: E.dma_start(
        out=ap(ccT, 0, [[16 * D, 128], [D, 16], [1, D]]),
        in_=ap(ccT_h, 0, [[D, 128], [128 * D, 16], [1, D]])),
        writes=[b_w["ccT"]])
    p.dma("g", lambda E: E.dma_start(
        out=ap(wsup, 0, [[NCH * FEXP, 128], [FEXP, NCH], [1, FEXP]]),
        in_=ap(wsup_h, 0, [[FEXP, 128], [128 * FEXP, NCH], [1, FEXP]])),
        writes=[b_w["wsup"]])
    p.dma("g", lambda E: E.dma_start(
        out=ap(wsdn, 0, [[4 * D, 128], [D, 4], [1, D]]),
        in_=ap(wsdn_h, 0, [[D, 128], [128 * D, 4], [1, D]])),
        writes=[b_w["wsdn"]])
    p.dma("q", lambda E: E.dma_start(
        out=ap(cent, 0, [[NCH * NEXP, 128], [NEXP, NCH], [1, NEXP]]),
        in_=ap(cent_h, 0, [[NEXP, 128], [128 * NEXP, NCH], [1, NEXP]])),
        writes=[b_w["cent"]])
    p.dma("g", lambda E: E.dma_start(out=projb[:, :], in_=proj_h[:, :]),
          writes=[b_w["proj"]])
    p.dma("g", lambda E: E.dma_start(out=onecb[:, :], in_=onec_h[:, :]),
          writes=[b_w["onecb"]])
    for hb, sbuf, key in [(cb_h, cb, "cb"), (bias_bc_h, bias_bc, "bias_bc"),
                          (onec_h, onec, "onec"), (ident_h, ident, "ident")]:
        p.dma("q", lambda E, hb=hb, sbuf=sbuf: E.dma_start(
            out=sbuf[:, :], in_=hb[:, :]), writes=[b_w[key]])

    def p1_load(j):
        par = j % 2
        js = j * TW
        for k in range(4):
            p.dma("q", lambda E, js=js, par=par, k=k: E.dma_start(
                out=ap(xsl[par], 2 * k * TW, [[NCH * TW, 128], [1, 2 * TW]]),
                in_=ap(x1_h, js * NCH + 2 * k * TW,
                       [[NTILE * NCH * TW, 128], [1, 2 * TW]])),
                writes=[b_xsl[par][k]])

    def p1_stats(j):
        par = j % 2
        xs, bxs = xsl[par], b_xsl[par]
        for ci in range(NCH):
            fn = lambda E, ci=ci, xs=xs: E.matmul(
                ps_su[0:1, :], onec[:, 0:1], xs[:, ci * TW:(ci + 1) * TW],
                start=(ci == 0), stop=(ci == NCH - 1))
            if ci < NCH - 1:
                p.gop("tensor", fn, reads=[b_w["onec"], bxs[ci // 2]], writes=[b_ps_su])
            else:
                p.op("tensor", fn, reads=[b_w["onec"], bxs[ci // 2]], writes=[b_ps_su])
        for ci in range(NCH):
            sl = ci % 2
            p.op("scalar", lambda E, ci=ci, sl=sl, xs=xs: E.activation(
                sq[:, sl * TW:(sl + 1) * TW], xs[:, ci * TW:(ci + 1) * TW],
                AF.Square), reads=[bxs[ci // 2]], writes=[b_sq[sl]])
            p.op("tensor", lambda E, ci=ci, sl=sl: E.matmul(
                ps_sq[0:1, :], onecb[:, 0:1], sq[:, sl * TW:(sl + 1) * TW],
                start=(ci == 0), stop=(ci == NCH - 1)),
                reads=[b_w["onecb"], b_sq[sl]], writes=[b_ps_sq])
        p.op("scalar", lambda E: E.activation(mean[0:1, :], ps_su[0:1, :],
                                              AF.Copy, scale=1.0 / D),
             reads=[b_ps_su, p.cst(1.0 / D)], writes=[b_stat])
        p.op("scalar", lambda E: E.activation(msq[0:1, :], ps_sq[0:1, :],
                                              AF.Copy, scale=1.0 / D),
             reads=[b_ps_sq, p.cst(1.0 / D)], writes=[b_stat])
        p.op("vector", lambda E: E.tensor_mul(var[0:1, :], mean[0:1, :],
                                              mean[0:1, :]), rw=[b_stat])
        p.op("vector", lambda E: E.tensor_sub(var[0:1, :], msq[0:1, :],
                                              var[0:1, :]), rw=[b_stat])
        p.op("scalar", lambda E: E.activation(varl[0:1, :], var[0:1, :],
                                              AF.Ln, bias=1e-5),
             reads=[p.cst(1e-5)], rw=[b_stat])
        p.op("scalar", lambda E: E.activation(rstd[0:1, :], varl[0:1, :],
                                              AF.Exp, scale=-0.5),
             reads=[p.cst(-0.5)], rw=[b_stat])
        p.op("vector", lambda E: E.tensor_mul(mrs[0:1, :], mean[0:1, :],
                                              rstd[0:1, :]), rw=[b_stat])
        p.dma("q", lambda E: E.dma_start(
            out=rstd_bc[:, :], in_=ap(rstd, 0, [[TW, 1], [0, 128], [1, TW]])),
            reads=[b_stat], writes=[b_bc])
        p.dma("q", lambda E: E.dma_start(
            out=mrs_bc[:, :], in_=ap(mrs, 0, [[TW, 1], [0, 128], [1, TW]])),
            reads=[b_stat], writes=[b_bc])

    def p1_apply(j):
        par = j % 2
        js = j * TW
        xs, bxs = xsl[par], b_xsl[par]
        xb, bxb = xnb, b_xnb
        for ci in range(NCH):
            half = ci // 4
            eng = "gpsimd" if half == 0 else "vector"
            bx = b_xn1f[half]
            p.op(eng, lambda E, ci=ci, xs=xs: E.tensor_mul(
                xn1f[:, ci * TW:(ci + 1) * TW], xs[:, ci * TW:(ci + 1) * TW],
                rstd_bc[:, :]), reads=[bxs[ci // 2], b_bc], writes=[bx])
            p.op(eng, lambda E, ci=ci: E.tensor_sub(
                xn1f[:, ci * TW:(ci + 1) * TW], xn1f[:, ci * TW:(ci + 1) * TW],
                mrs_bc[:, :]), reads=[b_bc], rw=[bx])
            p.op("scalar", lambda E, ci=ci, xb=xb: E.activation(
                xb[:, ci * TW:(ci + 1) * TW], xn1f[:, ci * TW:(ci + 1) * TW],
                AF.Copy), reads=[bx], writes=[bxb[ci // 4]])
        for h in range(2):
            p.dma("g", lambda E, js=js, xb=xb, h=h: E.dma_start(
                out=ap(xn1b_h, js + 4 * h * 128 * NT,
                       [[NT, 128], [128 * NT, 4], [1, TW]]),
                in_=ap(xb, 4 * h * TW, [[NCH * TW, 128], [TW, 4], [1, TW]])),
                reads=[bxb[h]], writes=[b_xn1b[j][h]])
        if j < 2:
            for ci in range(NCH):
                p.op("vector", lambda E, ci=ci, xs=xs: E.tensor_add(
                    xs[:, ci * TW:(ci + 1) * TW], xs[:, ci * TW:(ci + 1) * TW],
                    xn1f[:, ci * TW:(ci + 1) * TW]),
                    reads=[b_xn1f[ci // 4]], rw=[bxs[ci // 2]])

    def p1_routing(j):
        js = j * TW
        for st in range(4):
            so = st * 128
            for ci in range(NCH):
                fn = lambda E, ci=ci, so=so: E.matmul(
                    ps_aff[:, :], xn1f[:, ci * TW + so:ci * TW + so + 128],
                    cent[:, ci * NEXP:(ci + 1) * NEXP],
                    start=(ci == 0), stop=(ci == NCH - 1))
                if ci < NCH - 1:
                    p.gop("tensor", fn, reads=[b_xn1f[ci // 4], b_w["cent"]],
                          writes=[b_ps_aff])
                else:
                    p.op("tensor", fn, reads=[b_xn1f[ci // 4], b_w["cent"]],
                         writes=[b_ps_aff])
            p.op("vector", lambda E: E.tensor_add(
                aff[:, :], ps_aff[:, :], bias_bc[:, :]),
                reads=[b_ps_aff, b_w["bias_bc"]], writes=[b_aff])
            p.op("vector", lambda E: E.max(top8[:, :], aff[:, :]),
                 reads=[b_aff], writes=[b_aff])
            p.op("vector", lambda E: E.tensor_scalar(
                selm[:, :], aff[:, :], top8[:, 7:8], None, ALU.is_ge),
                reads=[b_aff], writes=[b_aff])
            p.op("vector", lambda E: E.tensor_sub(
                aff[:, :], aff[:, :], bias_bc[:, :]),
                reads=[b_w["bias_bc"]], rw=[b_aff])
            p.op("scalar", lambda E: E.activation(sgm[:, :], aff[:, :],
                                                  AF.Sigmoid),
                 reads=[b_aff], writes=[b_aff])
            p.op("vector", lambda E: E.tensor_mul(sgm[:, :], sgm[:, :],
                                                  selm[:, :]), rw=[b_aff])
            p.op("vector", lambda E: E.reduce_sum(wsum[:, :], sgm[:, :],
                                                  axis=AXX), rw=[b_aff])
            p.op("vector", lambda E: E.tensor_scalar(
                wsum[:, :], wsum[:, :], 1e-8, None, ALU.add), rw=[b_aff])
            p.op("vector", lambda E: E.reciprocal(wsum[:, :], wsum[:, :]),
                 rw=[b_aff])
            p.op("vector", lambda E: E.tensor_scalar(
                cwt[:, :], sgm[:, :], wsum[:, 0:1], None, ALU.mult),
                rw=[b_aff])
            p.dma("a", lambda E, js=js, so=so: E.dma_start(
                out=cwm_h[js + so:js + so + 128, :], in_=cwt[:, :]),
                reads=[b_aff], writes=[b_cwm[j]])

    def p1_ffn(j):
        par = j % 2
        js = j * TW
        xs, bxs = xsl[par], b_xsl[par]
        xb, bxb = xnb, b_xnb
        for fi in range(4):
            pm, bpm = (ps_m0, b_ps_m[0]) if fi % 2 == 0 else (ps_m1, b_ps_m[1])
            for ci in range(NCH):
                fn = lambda E, ci=ci, fi=fi, pm=pm, xb=xb: E.matmul(
                    pm[:, :], wsup[:, ci * FEXP + fi * 128:ci * FEXP + (fi + 1) * 128],
                    xb[:, ci * TW:(ci + 1) * TW],
                    start=(ci == 0), stop=(ci == NCH - 1))
                if ci < NCH - 1:
                    p.gop("tensor", fn, reads=[b_w["wsup"], bxb[ci // 4]], writes=[bpm])
                else:
                    p.op("tensor", fn, reads=[b_w["wsup"], bxb[ci // 4]], writes=[bpm])
            p.op("scalar", lambda E, fi=fi, pm=pm: E.activation(
                hsh[:, fi * TW:(fi + 1) * TW], pm[:, :], AF.Silu),
                reads=[bpm], writes=[b_hsh])
        for dj in range(NCH):
            pm, bpm = (ps_m0, b_ps_m[0]) if dj % 2 == 0 else (ps_m1, b_ps_m[1])
            for fi in range(4):
                fn = lambda E, fi=fi, dj=dj, pm=pm: E.matmul(
                    pm[:, :], wsdn[:, fi * D + dj * 128:fi * D + (dj + 1) * 128],
                    hsh[:, fi * TW:(fi + 1) * TW],
                    start=(fi == 0), stop=(fi == 3))
                if fi < 3:
                    p.gop("tensor", fn, reads=[b_w["wsdn"], b_hsh], writes=[bpm])
                else:
                    p.op("tensor", fn, reads=[b_w["wsdn"], b_hsh], writes=[bpm])
            p.op("vector", lambda E, dj=dj, pm=pm, xs=xs: E.scalar_tensor_tensor(
                xs[:, dj * TW:(dj + 1) * TW], pm[:, :], 1.0,
                xs[:, dj * TW:(dj + 1) * TW], ALU.mult, ALU.add),
                reads=[bpm], rw=[bxs[dj // 2]])
        for k, qq in enumerate(("a", "a", "g", "g")):
            p.dma(qq, lambda E, js=js, xs=xs, k=k: E.dma_start(
                out=ap(z_h, js + 2 * k * 128 * NT,
                       [[NT, 128], [128 * NT, 2], [1, TW]]),
                in_=ap(xs, 2 * k * TW, [[NCH * TW, 128], [TW, 2], [1, TW]])),
                reads=[bxs[k]], writes=[b_z[j][k]])

    p1_load(0)
    p1_stats(0)
    p1_apply(0)
    for j in range(NTILE):
        if j + 1 < NTILE:
            p1_load(j + 1)
        p1_ffn(j)
        if j + 1 < NTILE:
            p1_stats(j + 1)
        p1_routing(j)
        if j + 1 < NTILE:
            p1_apply(j + 1)

    def p2_stage1(pc):
        par = pc % 2
        po = pc * 128
        xq, bxq = xpc[par], b_xpc[par]
        hq, bhq = hpb[par], b_hpb[par]
        sq_, bsq = scs[par], b_scs[par]
        pssc, bpssc = ps_sc[par], b_ps_sc[par]
        for ci in range(NCH):
            p.dma("q", lambda E, po=po, ci=ci, xq=xq: E.dma_start(
                out=ap(xq, ci * NB * 128, [[NCH * NB * 128, 128], [128, NB], [1, 128]]),
                in_=ap(xn1b_h, ci * 128 * NT + po, [[NT, 128], [PL, NB], [1, 128]])),
                reads=[b for t in range(NTILE) for b in b_xn1b[t]], writes=[bxq])
        for s in range(7):
            for ki in range(NCH):
                pm, bpm = (ps_m0, b_ps_m[0]) if ki % 2 == 0 else (ps_m1, b_ps_m[1])
                for ci in range(NCH):
                    fn = lambda E, ci=ci, ki=ki, s=s, pm=pm, xq=xq: E.matmul(
                        pm[:, 0:128],
                        Wb[:, ci * D + ki * 128:ci * D + (ki + 1) * 128],
                        xq[:, ci * NB * 128 + s * 128:ci * NB * 128 + (s + 1) * 128],
                        start=(ci == 0), stop=(ci == NCH - 1))
                    if ci < NCH - 1:
                        p.gop("tensor", fn, reads=[b_w["W"], bxq], writes=[bpm])
                    else:
                        p.op("tensor", fn, reads=[b_w["W"], bxq], writes=[bpm])
                p.op("scalar", lambda E, ki=ki, pm=pm, hq=hq: E.activation(
                    hq[:, ki * 128:(ki + 1) * 128], pm[:, 0:128], AF.Tanh),
                    reads=[bpm], writes=[bhq])
            for ki in range(NCH):
                fn = lambda E, ki=ki, s=s, hq=hq, pssc=pssc: E.matmul(
                    pssc[:, s:s + 1], hq[:, ki * 128:(ki + 1) * 128],
                    projb[:, ki:ki + 1], start=(ki == 0), stop=(ki == NCH - 1))
                if ki < NCH - 1:
                    p.gop("tensor", fn, reads=[bhq, b_w["proj"]], writes=[bpssc])
                else:
                    p.op("tensor", fn, reads=[bhq, b_w["proj"]], writes=[bpssc])
            p.op("vector", lambda E, s=s, sq_=sq_, pssc=pssc: E.tensor_copy(
                sq_[:, s:s + 1], pssc[:, s:s + 1]),
                reads=[bpssc], writes=[bsq])

    def p2_s2a(pc):
        par = pc % 2
        po = pc * 128
        xq, bxq = xpc[par], b_xpc[par]
        sq_, bsq = scs[par], b_scs[par]
        p.op("vector", lambda E, sq_=sq_: E.reduce_max(
            emax[:, :], sq_[:, 0:7], axis=AXX), reads=[bsq], writes=[b_soft])
        p.op("vector", lambda E: E.tensor_scalar(
            emax[:, :], emax[:, :], -1.0, None, ALU.mult), rw=[b_soft])
        p.op("scalar", lambda E, sq_=sq_: E.activation(
            gex[:, :], sq_[:, 0:7], AF.Exp, bias=emax[:, 0:1]),
            reads=[bsq], rw=[b_soft])
        p.op("vector", lambda E: E.tensor_add(cums[:, 0:1], gex[:, 0:1],
                                              gex[:, 1:2]), rw=[b_soft])
        for t in range(3, 8):
            p.op("vector", lambda E, t=t: E.tensor_add(
                cums[:, t - 2:t - 1], cums[:, t - 3:t - 2], gex[:, t - 1:t]),
                rw=[b_soft])
        p.op("vector", lambda E: E.reciprocal(rnv[:, :], cums[:, :]),
             rw=[b_soft])
        p.op("vector", lambda E: E.tensor_copy(trin[:, 0:7], gex[:, :]),
             reads=[b_soft], writes=[b_trT])
        p.op("vector", lambda E: E.tensor_copy(trin[:, 7:13], rnv[:, :]),
             reads=[b_soft], writes=[b_trT])
        p.op("vector", lambda E: E.memset(trin[:, 13:16], 0.0), writes=[b_trT])
        p.op("tensor", lambda E: E.transpose(ps_tr[:, :], trin[:, 0:16],
                                             ident[:, :]),
             reads=[b_trT, b_w["ident"]], writes=[b_ps_tr])
        p.op("vector", lambda E: E.tensor_copy(trT[:, :], ps_tr[:, :]),
             reads=[b_ps_tr], writes=[b_trT])
        for s in range(7):
            p.dma("g", lambda E, s=s: E.dma_start(
                out=gbc[:, s * 128:(s + 1) * 128],
                in_=ap(trT, s * 128, [[128, 1], [0, 128], [1, 128]])),
                reads=[b_trT], writes=[b_gbc])
        for t in range(6):
            p.dma("q", lambda E, t=t: E.dma_start(
                out=rnbc[:, t * 128:(t + 1) * 128],
                in_=ap(trT, (7 + t) * 128, [[128, 1], [0, 128], [1, 128]])),
                reads=[b_trT], writes=[b_rnbc])
    def p2_s2b(pc):
        par = pc % 2
        po = pc * 128
        xq, bxq = xpc[par], b_xpc[par]
        for s in range(7):
            xin = lambda E, s=s, xq=xq: ap(
                xq, s * 128, [[NCH * NB * 128, 128], [NB * 128, NCH], [1, 128]])
            gin = lambda E, s=s: ap(gbc, s * 128, [[7 * 128, 128], [0, NCH], [1, 128]])
            if s == 0:
                p.op("vector", lambda E, s=s, xq=xq: E.tensor_mul(
                    ctxa[:, :],
                    ap(xq, s * 128, [[NCH * NB * 128, 128], [NB * 128, NCH], [1, 128]]),
                    ap(gbc, s * 128, [[7 * 128, 128], [0, NCH], [1, 128]])),
                    reads=[bxq, b_gbc], writes=[b_ctxa])
            else:
                p.op("vector", lambda E, s=s, xq=xq: E.tensor_mul(
                    tmpc[:, :],
                    ap(xq, s * 128, [[NCH * NB * 128, 128], [NB * 128, NCH], [1, 128]]),
                    ap(gbc, s * 128, [[7 * 128, 128], [0, NCH], [1, 128]])),
                    reads=[bxq, b_gbc], writes=[b_tmpc])
                p.op("vector", lambda E: E.tensor_add(
                    ctxa[:, :], ctxa[:, :], tmpc[:, :]),
                    reads=[b_tmpc], rw=[b_ctxa])
            t = s + 1
            if 2 <= t <= 7:
                p.op("vector", lambda E, t=t: E.tensor_mul(
                    ap(ctxb, (t - 2) * 128,
                       [[NCH * 6 * 128, 128], [6 * 128, NCH], [1, 128]]),
                    ctxa[:, :],
                    ap(rnbc, (t - 2) * 128, [[6 * 128, 128], [0, NCH], [1, 128]])),
                    reads=[b_ctxa, b_rnbc], writes=[b_ctxb])
        for t in range(2, 8):
            for hj in range(NCH):
                pm, bpm = (ps_m0, b_ps_m[0]) if hj % 2 == 0 else (ps_m1, b_ps_m[1])
                for c2 in range(16):
                    if c2 < 8:
                        rhs = lambda E, c2=c2, t=t, xq=xq: xq[:, c2 * NB * 128 + t * 128:
                                                             c2 * NB * 128 + (t + 1) * 128]
                        rb = bxq
                    else:
                        rhs = lambda E, c2=c2, t=t: ctxb[:, ((c2 - 8) * 6 + t - 2) * 128:
                                                         ((c2 - 8) * 6 + t - 1) * 128]
                        rb = b_ctxb
                    fn = lambda E, c2=c2, hj=hj, rhs=rhs, pm=pm: E.matmul(
                        pm[:, 0:128],
                        ccT[:, c2 * D + hj * 128:c2 * D + (hj + 1) * 128],
                        rhs(E), start=(c2 == 0), stop=(c2 == 15))
                    if c2 < 15:
                        p.gop("tensor", fn, reads=[b_w["ccT"], rb], writes=[bpm])
                    else:
                        p.op("tensor", fn, reads=[b_w["ccT"], rb], writes=[bpm])
                p.op("scalar", lambda E, hj=hj, pm=pm: E.activation(
                    attt[:, hj * 128:(hj + 1) * 128], pm[:, 0:128], AF.Tanh,
                    bias=cb[:, hj:hj + 1]),
                    reads=[bpm, b_w["cb"]], writes=[b_attt])
            p.dma("g", lambda E, t=t, po=po: E.dma_start(
                out=ap(z_h, t * PL + po, [[NT, 128], [128 * NT, NCH], [1, 128]]),
                in_=ap(attt, 0, [[NCH * 128, 128], [128, NCH], [1, 128]]),
                accum_op=ALU.add),
                reads=[b_attt], rw=b_z[t])

    p2_stage1(0)
    for pc in range(4):
        p2_s2a(pc)
        if pc + 1 < 4:
            p2_stage1(pc + 1)
        p2_s2b(pc)

    out_tokens = []
    for j in range(NTILE):
        for b in (*b_z[j], *b_xn1b[j], b_cwm[j]):
            if b.writer:
                out_tokens.append(b.writer)
    best = {}
    for s, v in out_tokens:
        best[s] = max(best.get(s, 0), v)
    return sorted(best.items())


def build_l1a():
    nc = bass.Bass("TRN2", target_bir_lowering=False, debug=False)
    ctx = ExitStack()
    T = {"x1": nc.dram_tensor("x1", [128, NTILE * NCH * TW], F32, kind="ExternalInput"),
         "onec": nc.dram_tensor("onec", [128, 1], F32, kind="ExternalInput")}
    _decl_l1a_drams(nc, fused=False, T=T)
    p = Prog(nc, ctx)
    toks = emit_l1a(nc, p, T, ctx)
    p.finalize(toks)
    ctx.close()
    return nc



NGRP = 2
GW = NT // NGRP
GTILES = GW // TW

def _decl_l1b_drams(nc, T=None):
    if T is None:
        T = {}
    T["wup"] = nc.dram_tensor("wup", [NEXP * D, FEXP], F32, kind="ExternalInput")
    T["wdn"] = nc.dram_tensor("wdn", [NEXP * FEXP, D], F32, kind="ExternalInput")
    T["out"] = nc.dram_tensor("out", [D, NT], F32, kind="ExternalOutput")
    T["cwTr"] = nc.dram_tensor("cwTr", [NEXP, NT], F32, kind="Internal")
    return T


def emit_l1b(nc, p, T, ctx, pre="l1b_"):
    z_h = T["z"]; xn1b_h = T["xn1b"]; cwm_h = T["cwm"]; wup_h = T["wup"]
    wdn_h = T["wdn"]; ident_h = T["ident"]; out_h = T["out"]; cwT_h = T["cwTr"]

    sb = lambda name, shape, dt: ctx.enter_context(
        nc.sbuf_tensor(pre + name, shape, dt))
    psb = lambda name, shape, dt=F32: ctx.enter_context(
        nc.psum_tensor(pre + name, shape, dt))

    NSLOT = 4
    wu = [sb(f"wu{k}", [128, NCH * FEXP], BF16) for k in range(NSLOT)]
    wd = [sb(f"wd{k}", [128, 4 * D], BF16) for k in range(NSLOT)]
    xng = sb("xng", [128, NCH * GW], BF16)
    accg = sb("accg", [128, NCH * GW], F32)
    hf = [[sb(f"hf{pr}_{k}", [128, 4 * TW], BF16) for k in range(2)]
          for pr in range(2)]
    htmp = sb("htmp", [128, TW], F32)
    cwbc = [sb(f"cwbc{k}", [128, TW], F32) for k in range(2)]
    zsl = sb("zsl", [128, NCH * TW], F32)
    cwtile = sb("cwtile", [128, NEXP], F32)
    cwev = sb("cwev", [NEXP, 128], F32)
    ident = sb("ident_s", [128, 128], F32)

    ps_u0 = psb("ps_u0", [128, TW])
    ps_u1 = psb("ps_u1", [128, TW])
    ps_d0 = psb("ps_d0", [128, TW])
    ps_d1 = psb("ps_d1", [128, TW])
    ps_t = psb("ps_t", [NEXP, 128])

    B = lambda n: Buf(n)
    b_ident = B("ident")
    b_wu = [B(f"wu{k}") for k in range(NSLOT)]
    b_wd = [B(f"wd{k}") for k in range(NSLOT)]
    b_xng = B("xng")
    b_accg = B("accg")
    b_hf = [[B(f"hf{pr}_{k}") for k in range(2)] for pr in range(2)]
    b_htmp = B("htmp")
    b_cwbc = [B(f"cwbc{k}") for k in range(2)]
    b_zsl = B("zsl")
    b_cwtile = B("cwtile")
    b_cwev = B("cwev")
    b_psu = [B("ps_u0"), B("ps_u1")]
    b_psd = [B("ps_d0"), B("ps_d1")]
    b_pst = B("ps_t")
    b_cwT = B("cwT_h")
    b_out = [B(f"out{j}") for j in range(NTILE)]

    p.dma("q", lambda E: E.dma_start(out=ident[:, :], in_=ident_h[:, :]),
          writes=[b_ident])

    for st in range(NT // 128):
        p.dma("q", lambda E, st=st: E.dma_start(
            out=cwtile[:, :], in_=cwm_h[st * 128:(st + 1) * 128, :]),
            writes=[b_cwtile])
        p.op("tensor", lambda E: E.transpose(ps_t[:, :], cwtile[:, :],
                                             ident[:, :]),
             reads=[b_cwtile, b_ident], writes=[b_pst])
        p.op("vector", lambda E: E.tensor_copy(cwev[:, :], ps_t[:, :]),
             reads=[b_pst], writes=[b_cwev])
        p.dma("a", lambda E, st=st: E.dma_start(
            out=ap(cwT_h, st * 128, [[NT, NEXP], [1, 128]]),
            in_=cwev[:, :]), reads=[b_cwev], writes=[b_cwT])

    def load_expert(e):
        k = e % NSLOT
        p.dma("g", lambda E, e=e, k=k: E.dma_start(
            out=ap(wu[k], 0, [[NCH * FEXP, 128], [FEXP, NCH], [1, FEXP]]),
            in_=ap(wup_h, e * D * FEXP,
                   [[FEXP, 128], [128 * FEXP, NCH], [1, FEXP]])),
            writes=[b_wu[k]])
        p.dma("g", lambda E, e=e, k=k: E.dma_start(
            out=ap(wd[k], 0, [[4 * D, 128], [D, 4], [1, D]]),
            in_=ap(wdn_h, e * FEXP * D, [[D, 128], [128 * D, 4], [1, D]])),
            writes=[b_wd[k]])

    def emit_final(g):
        go = g * GW
        for tile in range(GTILES):
            ts = go + tile * TW
            j = ts // TW
            p.dma("q", lambda E, ts=ts: E.dma_start(
                out=ap(zsl, 0, [[NCH * TW, 128], [TW, NCH], [1, TW]]),
                in_=ap(z_h, ts, [[NT, 128], [128 * NT, NCH], [1, TW]])),
                writes=[b_zsl])
            for dj in range(NCH):
                p.op("vector", lambda E, dj=dj, tile=tile: E.tensor_add(
                    zsl[:, dj * TW:(dj + 1) * TW],
                    zsl[:, dj * TW:(dj + 1) * TW],
                    accg[:, dj * GW + tile * TW:dj * GW + tile * TW + TW]),
                    reads=[b_accg], rw=[b_zsl])
            p.dma("a", lambda E, ts=ts: E.dma_start(
                out=ap(out_h, ts, [[NT, 128], [128 * NT, NCH], [1, TW]]),
                in_=ap(zsl, 0, [[NCH * TW, 128], [TW, NCH], [1, TW]])),
                reads=[b_zsl], writes=[b_out[j]])

    b_xngc = [Buf(f"xng{ci}") for ci in range(NCH)]
    npairs = NEXP // 2
    seq = [(g, pair) for g in range(NGRP) for pair in range(npairs)]
    for e in range(4):
        load_expert(e)

    def emit_down(dn):
        (e0, k0, e1, k1, par, tile, g, first) = dn
        go = g * GW
        for dj in range(NCH):
            pm, bpm = (ps_d0, b_psd[0]) if dj % 2 == 0 else (ps_d1, b_psd[1])
            nmm = 0
            for (e, k, hx) in [(e0, k0, 0), (e1, k1, 1)]:
                for fi in range(4):
                    nmm += 1
                    fn = lambda E, fi=fi, dj=dj, k=k, hx=hx, par=par, pm=pm, nmm=nmm: E.matmul(
                        pm[:, :],
                        wd[k][:, fi * D + dj * 128:fi * D + (dj + 1) * 128],
                        hf[par][hx][:, fi * TW:(fi + 1) * TW],
                        start=(nmm == 1), stop=(nmm == 8))
                    if nmm < 8:
                        p.gop("tensor", fn, reads=[b_wd[k], b_hf[par][hx]],
                              writes=[bpm])
                    else:
                        p.op("tensor", fn, reads=[b_wd[k], b_hf[par][hx]],
                             writes=[bpm])
            if first:
                p.op("vector", lambda E, dj=dj, tile=tile, go=go, pm=pm: E.tensor_copy(
                    accg[:, dj * GW + tile * TW:dj * GW + tile * TW + TW],
                    pm[:, :]), reads=[bpm], writes=[b_accg])
            else:
                p.op("vector", lambda E, dj=dj, tile=tile, go=go, pm=pm: E.scalar_tensor_tensor(
                    accg[:, dj * GW + tile * TW:dj * GW + tile * TW + TW],
                    pm[:, :], 1.0,
                    accg[:, dj * GW + tile * TW:dj * GW + tile * TW + TW],
                    ALU.mult, ALU.add), reads=[bpm], rw=[b_accg])

    pending = None
    upcnt = 0
    for qi, (g, pair) in enumerate(seq):
        go = g * GW
        if pair == 0:
            if pending is not None:
                emit_down(pending)
                pending = None
            if g > 0:
                emit_final(g - 1)
            for ci in range(NCH):
                p.dma("q", lambda E, ci=ci, go=go: E.dma_start(
                    out=ap(xng, ci * GW, [[NCH * GW, 128], [1, GW]]),
                    in_=ap(xn1b_h, ci * 128 * NT + go, [[NT, 128], [1, GW]])),
                    writes=[b_xngc[ci]])
        e0, e1 = 2 * pair, 2 * pair + 1
        k0, k1 = e0 % NSLOT, e1 % NSLOT
        for tile in range(GTILES):
            ts = go + tile * TW
            par = upcnt % 2
            upcnt += 1
            for (e, k, hx) in [(e0, k0, 0), (e1, k1, 1)]:
                p.dma("q", lambda E, e=e, ts=ts, hx=hx: E.dma_start(
                    out=cwbc[hx][:, :],
                    in_=ap(cwT_h, e * NT + ts, [[NT, 1], [0, 128], [1, TW]])),
                    reads=[b_cwT], writes=[b_cwbc[hx]])
                for fi in range(4):
                    pm, bpm = (ps_u0, b_psu[0]) if fi % 2 == 0 else (ps_u1, b_psu[1])
                    for ci in range(NCH):
                        fn = lambda E, ci=ci, fi=fi, k=k, tile=tile, go=go, pm=pm: E.matmul(
                            pm[:, :],
                            wu[k][:, ci * FEXP + fi * 128:ci * FEXP + (fi + 1) * 128],
                            xng[:, ci * GW + tile * TW:ci * GW + tile * TW + TW],
                            start=(ci == 0), stop=(ci == NCH - 1))
                        if ci < NCH - 1:
                            p.gop("tensor", fn, reads=[b_wu[k], b_xngc[ci]],
                                  writes=[bpm])
                        else:
                            p.op("tensor", fn, reads=[b_wu[k], b_xngc[ci]],
                                 writes=[bpm])
                    p.op("scalar", lambda E, pm=pm: E.activation(
                        htmp[:, :], pm[:, :], AF.Silu),
                        reads=[bpm], writes=[b_htmp])
                    p.op("vector", lambda E, fi=fi, hx=hx, par=par: E.tensor_mul(
                        hf[par][hx][:, fi * TW:(fi + 1) * TW], htmp[:, :],
                        cwbc[hx][:, :]),
                        reads=[b_htmp, b_cwbc[hx]], writes=[b_hf[par][hx]])
            if pending is not None:
                emit_down(pending)
            if tile == 0 and qi >= 1 and qi + 1 < len(seq):
                npair = seq[qi + 1][1]
                load_expert(2 * npair)
                load_expert(2 * npair + 1)
            pending = (e0, k0, e1, k1, par, tile, g, pair == 0)
    emit_down(pending)
    emit_final(NGRP - 1)

    best = {}
    for j in range(NTILE):
        if b_out[j].writer:
            s, v = b_out[j].writer
            best[s] = max(best.get(s, 0), v)
    return sorted(best.items())


def build_l1b():
    nc = bass.Bass("TRN2", target_bir_lowering=False, debug=False)
    ctx = ExitStack()
    T = {"z": nc.dram_tensor("z", [D, NT], F32, kind="ExternalInput"),
         "xn1b": nc.dram_tensor("xn1b", [D, NT], BF16, kind="ExternalInput"),
         "cwm": nc.dram_tensor("cwm", [NT, NEXP], F32, kind="ExternalInput"),
         "ident": nc.dram_tensor("ident", [128, 128], F32, kind="ExternalInput")}
    _decl_l1b_drams(nc, T=T)
    p = Prog(nc, ctx)
    toks = emit_l1b(nc, p, T, ctx)
    p.finalize(toks)
    ctx.close()
    return nc


def build_fused():
    nc = bass.Bass("TRN2", target_bir_lowering=False, debug=False)
    outer = ExitStack()
    T = _decl_l0_drams(nc, fused=True)
    _decl_l1a_drams(nc, fused=True, T=T)
    _decl_l1b_drams(nc, T=T)
    p = Prog(nc, outer)
    for v in (1.0 / D, 1e-5, -0.5, 512.0):
        p.cst(v)
    with ExitStack() as ph:
        emit_l0(nc, p, T, ph)
    p.barrier()
    with ExitStack() as ph:
        emit_l1a(nc, p, T, ph)
    p.barrier()
    with ExitStack() as ph:
        toks = emit_l1b(nc, p, T, ph)
    p.finalize(toks)
    outer.close()
    return nc



def _f(a):
    return np.ascontiguousarray(a, dtype=np.float32)


def prep_l0_inputs(inputs):
    x = np.asarray(inputs["L0_attn_x"])
    B_w = np.asarray(inputs["L0_attn_B_w"])
    C_w = np.asarray(inputs["L0_attn_C_w"])
    logA = np.asarray(inputs["L0_attn_log_A"])
    dpar = np.asarray(inputs["L0_attn_D_param"])
    W_r = np.asarray(inputs["L0_ffn_W_r"])
    W_k = np.asarray(inputs["L0_ffn_W_k"])
    W_v = np.asarray(inputs["L0_ffn_W_v"])
    mur = np.asarray(inputs["L0_ffn_mu_r"])
    muk = np.asarray(inputs["L0_ffn_mu_k"])

    shared = dict(
        bwT=_f(B_w.T),
        cwT=_f(C_w.T),
        wrT=_f(W_r.T), wkT=_f(W_k.T), wvT=_f(W_v.T),
        logA=_f(logA.reshape(128, 1)),
        dpar=_f(dpar.reshape(NCH, 128).T),
        mur=_f(mur.reshape(NCH, 128).T),
        muk=_f(muk.reshape(NCH, 128).T),
        onec=_f(np.ones((128, 1))),
    )
    in_maps = []
    for c in range(NCORE):
        m = dict(shared)
        xs = x[:, c * PL:(c + 1) * PL, :]
        xfm = xs.transpose(2, 0, 1).reshape(D, NT)
        m["x"] = _f(xfm.reshape(NCH, 128, NTILE, TW).transpose(1, 2, 0, 3)
                    .reshape(128, NTILE * NCH * TW))
        if c == 0:
            halo = np.zeros((NB, D), dtype=np.float32)
        else:
            halo = _f(x[:, c * PL - 1, :])
        m["halo"] = _f(halo.T.reshape(NCH, 128, NB).transpose(1, 0, 2)
                       .reshape(128, NCH * NB))
        hp = np.zeros((128, NCORE * NB), dtype=np.float32)
        if c > 0:
            for b in range(NB):
                hp[:, (c - 1) * NB + b] = 1.0
        m["hpick"] = hp
        in_maps.append(m)
    return in_maps


def prep_l1a_inputs(inputs, x1_fm_percore):
    W = np.asarray(inputs["L1_attn_W"])
    proj = np.asarray(inputs["L1_attn_proj"])
    ccw = np.asarray(inputs["L1_attn_concat_w"])
    cbv = np.asarray(inputs["L1_attn_concat_b"])
    cent = np.asarray(inputs["L1_ffn_centroids"])
    ebias = np.asarray(inputs["L1_ffn_expert_biases"])
    wsup = np.asarray(inputs["L1_ffn_Ws_up"])
    wsdn = np.asarray(inputs["L1_ffn_Ws_down"])
    shared = dict(
        W=_f(W),
        proj=_f(proj.reshape(NCH, 128).T),
        ccT=_f(ccw.T),
        cb=_f(cbv.reshape(NCH, 128).T),
        wsup=_f(wsup),
        wsdn=_f(wsdn),
        cent=_f(cent.T),
        bias_bc=_f(np.tile(ebias, (128, 1))),
        onec=_f(np.ones((128, 1))),
        ident=_f(np.eye(128)),
    )
    in_maps = []
    for c in range(NCORE):
        m = dict(shared)
        m["x1"] = _f(x1_fm_percore[c])
        in_maps.append(m)
    return in_maps


def prep_l1b_inputs(inputs, res1_list):
    wup = np.asarray(inputs["L1_ffn_W_up"])
    wdn = np.asarray(inputs["L1_ffn_W_down"])
    shared = dict(
        wup=_f(wup.reshape(NEXP * D, FEXP)),
        wdn=_f(wdn.reshape(NEXP * FEXP, D)),
        ident=_f(np.eye(128)),
    )
    in_maps = []
    for c in range(NCORE):
        m = dict(shared)
        m["z"] = np.asarray(res1_list[c]["z"])
        m["xn1b"] = np.asarray(res1_list[c]["xn1b"])
        m["cwm"] = np.asarray(res1_list[c]["cwm"])
        in_maps.append(m)
    return in_maps


def assemble_fm(res_list, key):
    out = np.empty((NB, NCORE * PL, D), dtype=np.float32)
    for c, r in enumerate(res_list):
        a = np.asarray(r[key])
        out[:, c * PL:(c + 1) * PL, :] = (
            a.reshape(D, NB, PL).transpose(1, 2, 0))
    return out


def prep_fused_inputs(inputs):
    maps0 = prep_l0_inputs(inputs)
    W = np.asarray(inputs["L1_attn_W"])
    proj = np.asarray(inputs["L1_attn_proj"])
    ccw = np.asarray(inputs["L1_attn_concat_w"])
    cbv = np.asarray(inputs["L1_attn_concat_b"])
    cent = np.asarray(inputs["L1_ffn_centroids"])
    ebias = np.asarray(inputs["L1_ffn_expert_biases"])
    wsup = np.asarray(inputs["L1_ffn_Ws_up"])
    wsdn = np.asarray(inputs["L1_ffn_Ws_down"])
    wup = np.asarray(inputs["L1_ffn_W_up"])
    wdn = np.asarray(inputs["L1_ffn_W_down"])
    shared = dict(
        W=_f(W),
        proj=_f(proj.reshape(NCH, 128).T),
        ccT=_f(ccw.T),
        cb=_f(cbv.reshape(NCH, 128).T),
        wsup=_f(wsup),
        wsdn=_f(wsdn),
        cent=_f(cent.T),
        bias_bc=_f(np.tile(ebias, (128, 1))),
        ident=_f(np.eye(128)),
        wup=_f(wup.reshape(NEXP * D, FEXP)),
        wdn=_f(wdn.reshape(NEXP * FEXP, D)),
    )
    for m in maps0:
        m.update(shared)
    return maps0


def kernel_unfused(**inputs):
    cores = list(range(NCORE))
    nc0 = build_l0()
    res0 = run_bass_kernel_spmd(nc0, prep_l0_inputs(inputs), core_ids=cores)
    x1pc = [np.asarray(r["x1"]) for r in res0.results]
    nc1 = build_l1a()
    res1 = run_bass_kernel_spmd(nc1, prep_l1a_inputs(inputs, x1pc),
                                core_ids=cores)
    nc2 = build_l1b()
    res2 = run_bass_kernel_spmd(nc2, prep_l1b_inputs(inputs, res1.results),
                                core_ids=cores)
    return assemble_fm(res2.results, "out")


def kernel(**inputs):
    cores = list(range(NCORE))
    nc = build_fused()
    res = run_bass_kernel_spmd(nc, prep_fused_inputs(inputs), core_ids=cores)
    return assemble_fm(res.results, "out")
